# version 01
# baseline (speedup 1.0000x reference)
import numpy as np
import ml_dtypes
import concourse.bass as bass
import concourse.mybir as mybir
from concourse.bass_utils import run_bass_kernel_spmd

F32 = mybir.dt.float32
BF16 = mybir.dt.bfloat16
I32 = mybir.dt.int32
AF = mybir.ActivationFunctionType
ALU = mybir.AluOpType
AX = mybir.AxisListType
BF = ml_dtypes.bfloat16

D = 2048
T = 2048
HD = 64
HID = 5632
NT = 16
NS = 8
NHC = HID // 128
NDC = D // 128


class Ctr:
    def __init__(self, handle):
        self.h = handle
        self.n = 0

    def inc(self, ins, k=1):
        ins.then_inc(self.h, k)
        self.n += k
        return self.n


class _RecIns:
    __slots__ = ("incs",)

    def __init__(self):
        self.incs = []

    def then_inc(self, sem, k):
        self.incs.append((sem, k))
        return self


class _Rec:

    def __init__(self):
        self.ops = []

    def __getattr__(self, name):
        def call(*a, **k):
            h = _RecIns()
            self.ops.append((name, a, k, h))
            return h
        return call


def _replay(eng, rec):
    for name, a, k, h in rec.ops:
        ins = getattr(eng, name)(*a, **k)
        for sem, kk in h.incs:
            ins.then_inc(sem, kk)


def _w(eng, ctr, val):
    if ctr is not None and val > 0:
        eng.wait_ge(ctr.h, val)


def _wc(eng, c):
    if c is not None and c[0] is not None and c[1] > 0:
        eng.wait_ge(c[0].h, c[1])


class _StopEmit(Exception):
    pass


def build_nc(debug=False, stop_after=None):
    nc = bass.Bass("TRN2", target_bir_lowering=False)

    def din(name, shape, dt):
        return nc.dram_tensor(name, shape, dt, kind="ExternalInput").ap()

    xkv = din("xkv", [T, D], F32)
    wqt = din("wqt", [D, D], F32)
    wkt = din("wkt", [D, 512], F32)
    wvt = din("wvt", [D, 512], F32)
    wot = din("wot", [D, D], F32)
    gt = din("gt", [D, HID], I32)
    ut = din("ut", [D, HID], I32)
    dnt = din("dnt", [HID, D], I32)
    sgup = din("sgup", [128, 88], F32)
    sdn = din("sdn", [128, 16], F32)
    cost = din("cost", [T, 32], F32)
    sint = din("sint", [T, 32], F32)
    idn = din("idn", [128, 128], BF16)
    dmask = din("dmask", [128, 128], BF16)
    zflag = din("zflag", [128, 1], F32)
    yout = nc.dram_tensor("yout", [NS * 128, D], F32, kind="ExternalOutput").ap()
    dbgout = {}
    if debug:
        dbgout["h1fm"] = nc.dram_tensor("dbg_h1fm", [128, 32768], BF16, kind="ExternalOutput").ap()
        dbgout["kd"] = nc.dram_tensor("dbg_kd", [128, 16384], BF16, kind="ExternalOutput").ap()
        dbgout["va"] = nc.dram_tensor("dbg_va", [128, 8320], BF16, kind="ExternalOutput").ap()
        dbgout["qfm"] = nc.dram_tensor("dbg_qfm", [128, 16384], BF16, kind="ExternalOutput").ap()
        dbgout["capb"] = nc.dram_tensor("dbg_capb", [128, 32], F32, kind="ExternalOutput").ap()
        dbgout["yfm"] = nc.dram_tensor("dbg_yfm", [128, 16384], BF16, kind="ExternalOutput").ap()
        dbgout["x2"] = nc.dram_tensor("dbg_x2", [128, NS * D], F32, kind="ExternalOutput").ap()
        dbgout["h2fm"] = nc.dram_tensor("dbg_h2fm", [128, 16384], BF16, kind="ExternalOutput").ap()

    from contextlib import ExitStack
    with ExitStack() as _st:
        A64 = _st.enter_context(nc.sbuf_tensor("A64", [128, 32768], BF16))
        AKV = _st.enter_context(nc.sbuf_tensor("AKV", [128, 24704], BF16))
        AQ = _st.enter_context(nc.sbuf_tensor("AQ", [128, 16384], BF16))
        WBUF = _st.enter_context(nc.sbuf_tensor("WBUF", [128, 8192], BF16))
        SCR = _st.enter_context(nc.sbuf_tensor("SCR", [128, 4096], F32))
        ATN = _st.enter_context(nc.sbuf_tensor("ATN", [128, 2048], BF16))
        SQS = _st.enter_context(nc.sbuf_tensor("SQS", [128, 2048], BF16))
        H1T = _st.enter_context(nc.sbuf_tensor("H1T", [128, 4096], BF16))
        CID = _st.enter_context(nc.sbuf_tensor("CID", [128, 128], BF16))
        CMSK = _st.enter_context(nc.sbuf_tensor("CMSK", [128, 128], BF16))
        CONEB = _st.enter_context(nc.sbuf_tensor("CONEB", [128, 128], BF16))
        CONEF = _st.enter_context(nc.sbuf_tensor("CONEF", [128, 64], F32))
        CCOS = _st.enter_context(nc.sbuf_tensor("CCOS", [128, 512], F32))
        CSIN = _st.enter_context(nc.sbuf_tensor("CSIN", [128, 512], F32))
        CSG = _st.enter_context(nc.sbuf_tensor("CSG", [128, 88], F32))
        CSD = _st.enter_context(nc.sbuf_tensor("CSD", [128, 16], F32))
        CZF = _st.enter_context(nc.sbuf_tensor("CZF", [128, 1], F32))
        CEPS = _st.enter_context(nc.sbuf_tensor("CEPS", [128, 1], F32))
        CM128 = _st.enter_context(nc.sbuf_tensor("CM128", [128, 1], F32))
        STT = _st.enter_context(nc.sbuf_tensor("STT", [128, 32], F32))
        SM1 = _st.enter_context(nc.sbuf_tensor("SM1", [128, 8], F32))
        QM2 = _st.enter_context(nc.sbuf_tensor("QM2", [128, 32], F32))
        KM2 = _st.enter_context(nc.sbuf_tensor("KM2", [128, 8], F32))
        QM2B = _st.enter_context(nc.sbuf_tensor("QM2B", [128, 32], BF16))
        KM32B = _st.enter_context(nc.sbuf_tensor("KM32B", [128, 32], BF16))
        CAPR = _st.enter_context(nc.sbuf_tensor("CAPR", [1, 32], BF16))
        CAPB = _st.enter_context(nc.sbuf_tensor("CAPB", [128, 32], F32))
        CAPP = _st.enter_context(nc.sbuf_tensor("CAPP", [128, 16], F32))
        CAPZ = _st.enter_context(nc.sbuf_tensor("CAPZ", [128, 16], F32))
        LROW = _st.enter_context(nc.sbuf_tensor("LROW", [128, 1024], F32))
        PSAB0 = _st.enter_context(nc.psum_tensor("PSAB0", [128, 1024], F32))
        PSAB1 = _st.enter_context(nc.psum_tensor("PSAB1", [128, 1024], F32))
        PY0 = _st.enter_context(nc.psum_tensor("PY0", [128, 512], F32))
        PY1 = _st.enter_context(nc.psum_tensor("PY1", [128, 512], F32))
        PR0 = _st.enter_context(nc.psum_tensor("PR0", [128, 512], F32))
        PT = _st.enter_context(nc.psum_tensor("PT", [128, 1024], BF16))
        _dY = _st.enter_context(nc.semaphore("dY"))
        _dG = _st.enter_context(nc.semaphore("dG"))
        _dS = _st.enter_context(nc.semaphore("dS"))
        _cT = _st.enter_context(nc.semaphore("cT"))
        _cV = _st.enter_context(nc.semaphore("cV"))
        _cS = _st.enter_context(nc.semaphore("cS"))
        real_block = _st.enter_context(nc.Block())

        dY, dG, dS = Ctr(_dY), Ctr(_dG), Ctr(_dS)
        cT, cV, cS = Ctr(_cT), Ctr(_cV), Ctr(_cS)

        RECS = {n: _Rec() for n in ("sync", "gpsimd", "tensor", "vector", "scalar")}

        class _Shim:
            def __getattr__(self, name):
                rec = RECS[name]

                def run(fn):
                    fn(rec)
                return run

        block = _Shim()

        def _ss(eng, ctr, ins):
            c = ctr.inc(ins)
            eng.wait_ge(ctr.h, c)

        LA0 = SCR[0:1, 0:512]
        LB0 = SCR[0:1, 512:1024]
        LINV = SCR[:, 1024:1536]
        BST = SCR[:, 1536:1792].bitcast(BF16)

        PR1f = PT[:].bitcast(F32)
        PT2 = PR0[:].bitcast(BF16)

        PJ = [PSAB0[:, 0:512], PSAB0[:, 512:1024], PSAB1[:, 0:512], PSAB1[:, 512:1024],
              PY0[:], PY1[:], PR0[:], PR1f]

        pt_g = {"g": 0, "cons": [[], []]}

        def pt_open(t):
            b = pt_g["g"] % 2
            pt_g["g"] += 1
            for c in pt_g["cons"][b]:
                _wc(t, c)
            pt_g["cons"][b] = []
            return b, (PT if b == 0 else PT2)

        def a64_h1(c, t0, n):
            return A64[:, c * T + t0: c * T + t0 + n]

        def a64_y(c, t0, n, r0=0, r1=128):
            return A64[r0:r1, c * 1024 + t0: c * 1024 + t0 + n]

        def a64_hid(hc, t0, n):
            return A64[:, hc * 512 + t0: hc * 512 + t0 + n]

        def a64_ffb(dc, j0, n):
            return A64[:, 22528 + dc * 512 + j0: 22528 + dc * 512 + j0 + n]

        def akv_kd(g, k0, n, r0=0, r1=128):
            return AKV[r0:r1, g * 2048 + k0: g * 2048 + k0 + n]

        def akv_va(i, g):
            o = 16384 + i * 520 + g * 65
            return AKV[:, o: o + 65]

        AKVf = AKV[:].bitcast(F32)

        def x2loc(tau):
            if tau < 6:
                return AKVf[:, tau * 2048: (tau + 1) * 2048]
            return SCR[:, (tau - 6) * 2048: (tau - 5) * 2048]

        def aq_q(c, t0, n, r0=0, r1=128):
            return AQ[r0:r1, c * 1024 + t0: c * 1024 + t0 + n]

        OUTTILE = H1T[:].bitcast(F32)

        def _maybe_stop(tag):
            if stop_after == tag:
                def pj(s):
                    _w(s, cT, cT.n)
                    _w(s, cV, cV.n)
                    _w(s, cS, cS.n)
                    _w(s, dY, dY.n)
                    _w(s, dG, dG.n)
                    _w(s, dS, dS.n)
                    if debug:
                        for dst, src in [(dbgout["h1fm"][:], A64[:, 0:32768]),
                                         (dbgout["kd"][:], AKV[:, 0:16384]),
                                         (dbgout["va"][:], AKV[:, 16384:24704]),
                                         (dbgout["qfm"][:], AQ[:, 0:16384]),
                                         (dbgout["capb"][:], CAPB[:])]:
                            s.dma_start(out=dst, in_=src).then_inc(dY.h, 16)
                            dY.n += 16
                    s.wait_ge(dY.h, dY.n)

                block.sync(pj)
                raise _StopEmit()

        def emit_tr16(src_fn, dst_fn, pre_wait=None):
            res = {}
            for half in range(2):
                bpeek = pt_g["g"] % 2
                BK = PT if bpeek == 0 else PT2

                def p_tr(t, half=half):
                    if pre_wait is not None and half == 0:
                        pre_wait(t)
                    _b, BKx = pt_open(t)
                    for c8 in range(8):
                        c = half * 8 + c8
                        ins = t.transpose(BKx[:, c8 * 128:(c8 + 1) * 128], src_fn(c), CID[:])
                        cT.inc(ins)

                block.tensor(p_tr)
                close = cT.n

                if half == 0:
                    def p_ev_v(v, half=half, BK=BK, close=close):
                        _w(v, cT, close)
                        for c8 in range(8):
                            ins = v.tensor_copy(out=dst_fn(half * 8 + c8),
                                                in_=BK[:, c8 * 128:(c8 + 1) * 128])
                            cV.inc(ins)

                    block.vector(p_ev_v)
                    pt_g["cons"][bpeek] = [(cV, cV.n)]
                else:
                    def p_ev_s(s, half=half, BK=BK, close=close):
                        _w(s, cT, close)
                        for c8 in range(8):
                            ins = s.copy(out=dst_fn(half * 8 + c8),
                                         in_=BK[:, c8 * 128:(c8 + 1) * 128])
                            cS.inc(ins)

                    block.scalar(p_ev_s)
                    pt_g["cons"][bpeek] = [(cS, cS.n)]
            res["ct"] = cT.n
            res["cv"] = cV.n
            res["cs"] = cS.n
            return res

        rope_tmps = [LROW[:, 256 * j: 256 * (j + 1)].rearrange("p (h j) -> p h j", j=32)
                     for j in range(4)]

        def emit_rope(v, bank, out_bf, i):
            P3v = bank.rearrange("p (h j two) -> p h j two", j=32, two=2)
            Cb = CCOS[:, i * 32:(i + 1) * 32].unsqueeze(1).broadcast_to([128, 8, 32])
            Sb = CSIN[:, i * 32:(i + 1) * 32].unsqueeze(1).broadcast_to([128, 8, 32])
            T1, T2, T3, T4 = rope_tmps
            A3 = out_bf.rearrange("p (h j two) -> p h j two", j=32, two=2)
            v.tensor_tensor(out=T1, in0=P3v[:, :, :, 0], in1=Cb, op=ALU.mult)
            v.tensor_tensor(out=T2, in0=P3v[:, :, :, 1], in1=Cb, op=ALU.mult)
            v.tensor_tensor(out=T3, in0=P3v[:, :, :, 0], in1=Sb, op=ALU.mult)
            _ss(v, cV, v.tensor_tensor(out=T4, in0=P3v[:, :, :, 1], in1=Sb, op=ALU.mult))
            v.tensor_sub(out=A3[:, :, :, 0], in0=T1, in1=T4)
            return v.tensor_add(out=A3[:, :, :, 1], in0=T3, in1=T2)

        NSQ = ATN[:, 1024:1536]

        try:
            def p0_sync(s):
                for dst, src in [(CID[:], idn[:]), (CMSK[:], dmask[:]), (CSG[:], sgup[:]),
                                 (CSD[:], sdn[:]), (CZF[:], zflag[:])]:
                    s.dma_start(out=dst, in_=src).then_inc(dY.h, 16)
                    dY.n += 16
                s.dma_start(out=CCOS[:].rearrange("p (i j) -> p i j", j=32),
                            in_=cost.rearrange("(i p) j -> p i j", p=128)).then_inc(dY.h, 16)
                s.dma_start(out=CSIN[:].rearrange("p (i j) -> p i j", j=32),
                            in_=sint.rearrange("(i p) j -> p i j", p=128)).then_inc(dY.h, 16)
                dY.n += 32

            block.sync(p0_sync)
            CONST_DY = dY.n

            def p0_g(g):
                g.memset(CONEB[:], 1.0)
                g.memset(CONEF[:], 1.0)
                g.memset(CEPS[:], 1e-5)
                g.memset(CM128[:], -128.0)
                ap = AKV[:, 16384:24704].rearrange("p (i r) -> p i r", r=520)[:, :, 64:520:65]
                ins = g.memset(ap, 1.0)
                ins.then_inc(dG.h, 1)
                dG.n += 1

            block.gpsimd(p0_g)
            MEMSET_DG = dG.n
            _maybe_stop('p0')

            h1_done = [None] * NT
            h1t_used = [0] * NT
            xslot_free = [0, 0]
            norm_done = [0] * NT

            for i in range(NT):
                slot = i % 2
                xt = SCR[:, slot * 2048:(slot + 1) * 2048]
                c_ss = STT[:, 4 * slot + 0: 4 * slot + 1]
                c_sx = STT[:, 4 * slot + 1: 4 * slot + 2]
                c_iv = STT[:, 4 * slot + 2: 4 * slot + 3]
                c_nb = STT[:, 4 * slot + 3: 4 * slot + 4]

                def p1_load(s, i=i, xt=xt, slot=slot):
                    _w(s, cS, xslot_free[slot])
                    s.dma_start(out=xt, in_=xkv[i * 128:(i + 1) * 128, :]).then_inc(dY.h, 16)
                    dY.n += 16

                block.sync(p1_load)
                ld = dY.n

                def p1_stats(s, xt=xt, c_ss=c_ss, c_sx=c_sx, ld=ld, i=i):
                    _w(s, dY, ld)
                    if i == 0:
                        _w(s, dG, MEMSET_DG)
                    s.activation(out=SQS[:], in_=xt, func=AF.Square, accum_out=c_ss)
                    ins = s.activation(out=SQS[:], in_=xt, func=AF.Copy, accum_out=c_sx)
                    cS.inc(ins)

                block.scalar(p1_stats)
                stc = cS.n

                def p1_small(v, c_ss=c_ss, c_sx=c_sx, slot=slot, stc=stc):
                    _w(v, cS, stc)
                    m = SM1[:, slot:slot + 1]
                    t1 = SM1[:, 2 + slot:3 + slot]
                    v.tensor_scalar_mul(out=m, in0=c_sx, scalar1=1.0 / D)
                    _ss(v, cV, v.tensor_scalar_mul(out=t1, in0=c_ss, scalar1=1.0 / D))
                    _ss(v, cV, v.scalar_tensor_tensor(out=c_ss, in0=m, scalar=-1.0, in1=m,
                                                      op0=ALU.mult, op1=ALU.mult))
                    ins = v.tensor_add(out=c_ss, in0=c_ss, in1=t1)
                    cV.inc(ins)

                block.vector(p1_small)
                smc = cV.n

                def p1_sqrt(s, c_ss=c_ss, c_iv=c_iv, smc=smc):
                    _w(s, cV, smc)
                    ins = s.activation(out=c_iv, in_=c_ss, func=AF.Sqrt, bias=CEPS[:, 0:1])
                    cS.inc(ins)

                block.scalar(p1_sqrt)
                sqc = cS.n

                def p1_inv(v, c_iv=c_iv, c_nb=c_nb, slot=slot, sqc=sqc):
                    _w(v, cS, sqc)
                    _ss(v, cV, v.reciprocal(out=c_iv, in_=c_iv))
                    m = SM1[:, slot:slot + 1]
                    ins = v.scalar_tensor_tensor(out=c_nb, in0=m, scalar=-1.0, in1=c_iv,
                                                 op0=ALU.mult, op1=ALU.mult)
                    cV.inc(ins)

                block.vector(p1_inv)
                ivc = cV.n

                ht = H1T[:, (i % 2) * 2048:((i % 2) + 1) * 2048]

                def p1_norm(s, xt=xt, ht=ht, c_iv=c_iv, c_nb=c_nb, ivc=ivc, i=i):
                    _w(s, cV, ivc)
                    if i >= 2:
                        _w(s, cT, h1t_used[i - 2])
                    ins = s.activation(out=ht, in_=xt, func=AF.Identity, scale=c_iv, bias=c_nb)
                    cS.inc(ins)

                block.scalar(p1_norm)
                norm_done[i] = cS.n
                xslot_free[slot] = cS.n
                if i == 0:
                    _maybe_stop('p1_norm0')

                def p1_pre(t, i=i, nd=norm_done[i]):
                    _w(t, cS, nd)
                    if i == 0:
                        _w(t, dY, CONST_DY)

                r = emit_tr16(lambda c, ht=ht: ht[:, c * 128:(c + 1) * 128],
                              lambda c, i=i: a64_h1(c, i * 128, 128), p1_pre)
                h1t_used[i] = r["ct"]
                h1_done[i] = (r["cv"], r["cs"])
                if i == 0:
                    _maybe_stop('p1_tile0')
            _maybe_stop('p1')

            def p2_wk(g):
                g.dma_start(out=WBUF[:, 0:8192].rearrange("p (c j) -> p c j", j=512),
                            in_=wkt.rearrange("(c p) j -> p c j", p=128)).then_inc(dG.h, 16)
                dG.n += 16

            block.gpsimd(p2_wk)
            wk_ld = dG.n

            kproj_mm = [0] * NT
            kbank_free = [None, None]
            kd_ev_done = [0] * NT

            for i in range(NT):
                bank = PJ[i % 2]
                bidx = i % 2

                def p2_mm(t, i=i, bank=bank, bidx=bidx):
                    _w(t, dG, wk_ld)
                    hv, hs = h1_done[i]
                    _w(t, cV, hv)
                    _w(t, cS, hs)
                    _wc(t, kbank_free[bidx])
                    for c in range(16):
                        ins = t.matmul(bank, a64_h1(c, i * 128, 128),
                                       WBUF[:, c * 512:(c + 1) * 512], start=(c == 0),
                                       stop=(c == 15))
                    cT.inc(ins)

                block.tensor(p2_mm)
                kproj_mm[i] = cT.n

                aslot = ATN[:, (i % 2) * 512:((i % 2) + 1) * 512]

                def p2_rope(v, i=i, bank=bank, aslot=aslot, mc=kproj_mm[i]):
                    _w(v, cT, mc)
                    if i == 0:
                        _w(v, dY, CONST_DY)
                    if i >= 2:
                        _w(v, cS, kd_ev_done[i - 2])
                    _ss(v, cV, emit_rope(v, bank, aslot, i))
                    _ss(v, cV, v.scalar_tensor_tensor(out=NSQ, in0=aslot, scalar=1.0, in1=aslot,
                                                      op0=ALU.mult, op1=ALU.mult))
                    _ss(v, cV, v.reduce_sum(out=SM1[:, 0:8],
                                            in_=NSQ.rearrange("p (h j) -> p h j", j=64), axis=AX.X))
                    if i == 0:
                        ins = v.tensor_copy(out=KM2[:], in_=SM1[:, 0:8])
                    else:
                        ins = v.tensor_max(out=KM2[:], in0=KM2[:], in1=SM1[:, 0:8])
                    cV.inc(ins)

                block.vector(p2_rope)
                rp = cV.n
                kbank_free[bidx] = (cV, rp)

                bpeek = pt_g["g"] % 2
                BKkd = PT if bpeek == 0 else PT2

                def p2_kd_tr(t, aslot=aslot, rp=rp):
                    _w(t, cV, rp)
                    _b, BKx = pt_open(t)
                    for g8 in range(8):
                        t.transpose(BKx[0:64, g8 * 128:(g8 + 1) * 128],
                                    aslot[:, g8 * 64:(g8 + 1) * 64], CID[:])
                        ins = t.transpose(BKx[64:128, g8 * 128:(g8 + 1) * 128],
                                          aslot[:, g8 * 64:(g8 + 1) * 64], CID[:],
                                          tile_position=(0, 64))
                        cT.inc(ins)

                block.tensor(p2_kd_tr)
                close = cT.n

                def p2_kd_ev(s, i=i, BKkd=BKkd, close=close):
                    _w(s, cT, close)
                    for g8 in range(8):
                        ins = s.copy(out=akv_kd(g8, i * 128, 128),
                                     in_=BKkd[:, g8 * 128:(g8 + 1) * 128])
                        cS.inc(ins)

                block.scalar(p2_kd_ev)
                pt_g["cons"][bpeek] = [(cS, cS.n)]
                kd_ev_done[i] = cS.n
            kd_done_s = cS.n
            _maybe_stop('p2k')

            def p2_wv(g):
                _w(g, cT, kproj_mm[NT - 1])
                g.dma_start(out=WBUF[:, 0:8192].rearrange("p (c j) -> p c j", j=512),
                            in_=wvt.rearrange("(c p) j -> p c j", p=128)).then_inc(dG.h, 16)
                dG.n += 16

            block.gpsimd(p2_wv)
            wv_ld = dG.n
            vbank_free = [None, None]
            vproj_mm = [0] * NT
            for i in range(NT):
                bank = PJ[2 + i % 2]
                bidx = i % 2

                def p2v_mm(t, i=i, bank=bank, bidx=bidx):
                    _w(t, dG, wv_ld)
                    _wc(t, vbank_free[bidx])
                    for c in range(16):
                        ins = t.matmul(bank, a64_h1(c, i * 128, 128),
                                       WBUF[:, c * 512:(c + 1) * 512], start=(c == 0),
                                       stop=(c == 15))
                    cT.inc(ins)

                block.tensor(p2v_mm)
                vproj_mm[i] = cT.n

                def p2v_ev(v, i=i, bank=bank, mc=vproj_mm[i]):
                    _w(v, cT, mc)
                    if i == 0:
                        _w(v, dG, MEMSET_DG)
                    dst = AKV[:, 16384 + i * 520: 16384 + (i + 1) * 520].rearrange(
                        "p (g j) -> p g j", j=65)[:, :, 0:64]
                    ins = v.tensor_copy(out=dst, in_=bank.rearrange("p (g j) -> p g j", j=64))
                    cV.inc(ins)

                block.vector(p2v_ev)
                vbank_free[bidx] = (cV, cV.n)
            va_done = cV.n
            _maybe_stop('p2')

            wq_free = (cT, vproj_mm[NT - 1])
            qbank_free = [kbank_free[0], kbank_free[1]]
            qtr_done = [0, 0]
            for q in range(4):
                def p3_wq(g, q=q, wq_free=wq_free):
                    _wc(g, wq_free)
                    g.dma_start(out=WBUF[:, 0:8192].rearrange("p (c j) -> p c j", j=512),
                                in_=wqt[:, q * 512:(q + 1) * 512].rearrange("(c p) j -> p c j",
                                                                            p=128)
                                ).then_inc(dG.h, 16)
                    dG.n += 16

                block.gpsimd(p3_wq)
                wq_ld = dG.n

                for tau in range(NS):
                    i = 2 * tau
                    bank = PJ[tau % 2]
                    bidx = tau % 2

                    def p3_mm(t, i=i, bank=bank, bidx=bidx, wq_ld=wq_ld):
                        _w(t, dG, wq_ld)
                        _wc(t, qbank_free[bidx])
                        for cc in range(16):
                            ins = t.matmul(bank, a64_h1(cc, i * 128, 128),
                                           WBUF[:, cc * 512:(cc + 1) * 512],
                                           start=(cc == 0), stop=(cc == 15))
                        cT.inc(ins)

                    block.tensor(p3_mm)
                    qmm = cT.n

                    qb = H1T[:, (tau % 2) * 2048:(tau % 2) * 2048 + 512]

                    def p3_rope(v, q=q, tau=tau, i=i, bank=bank, qb=qb, mc=qmm):
                        _w(v, cT, mc)
                        _w(v, cT, qtr_done[tau % 2])
                        _ss(v, cV, emit_rope(v, bank, qb, i))
                        _ss(v, cV, v.scalar_tensor_tensor(out=NSQ, in0=qb, scalar=1.0, in1=qb,
                                                          op0=ALU.mult, op1=ALU.mult))
                        _ss(v, cV, v.reduce_sum(out=SM1[:, 0:8],
                                                in_=NSQ.rearrange("p (h j) -> p h j", j=64),
                                                axis=AX.X))
                        if tau == 0:
                            ins = v.tensor_copy(out=QM2[:, 8 * q:8 * (q + 1)], in_=SM1[:, 0:8])
                        else:
                            ins = v.tensor_max(out=QM2[:, 8 * q:8 * (q + 1)],
                                               in0=QM2[:, 8 * q:8 * (q + 1)], in1=SM1[:, 0:8])
                        cV.inc(ins)

                    block.vector(p3_rope)
                    rp = cV.n
                    qbank_free[bidx] = (cV, rp)

                    bpeek = pt_g["g"] % 2
                    BKq = PT if bpeek == 0 else PT2

                    def p3_tr(t, qb=qb, rp=rp):
                        _w(t, cV, rp)
                        _b, BKx = pt_open(t)
                        for j in range(4):
                            ins = t.transpose(BKx[:, j * 128:(j + 1) * 128],
                                              qb[:, j * 128:(j + 1) * 128], CID[:])
                            cT.inc(ins)

                    block.tensor(p3_tr)
                    qtr_done[tau % 2] = cT.n
                    close = cT.n

                    def p3_ev(v, q=q, tau=tau, BKq=BKq, close=close):
                        _w(v, cT, close)
                        for j in range(4):
                            ins = v.tensor_copy(out=aq_q(4 * q + j, tau * 128, 128),
                                                in_=BKq[:, j * 128:(j + 1) * 128])
                            cV.inc(ins)

                    block.vector(p3_ev)
                    pt_g["cons"][bpeek] = [(cV, cV.n)]
                wq_free = (cT, cT.n)
            qfm_done = cV.n
            p3_end_ct = cT.n
            _maybe_stop('p3q')

            def p4_v1(v):
                _w(v, cV, qfm_done)
                v.tensor_copy(out=QM2B[:], in_=QM2[:])
                ins = v.tensor_copy(
                    out=KM32B[:].rearrange("p (a b) -> p a b", b=4),
                    in_=KM2[:].unsqueeze(2).broadcast_to([128, 8, 4]))
                cV.inc(ins)

            block.vector(p4_v1)
            c1 = cV.n

            b41 = pt_g["g"] % 2
            BK41 = PT if b41 == 0 else PT2

            def p4_t1(t, c1=c1):
                _w(t, cV, c1)
                _b, BKx = pt_open(t)
                t.transpose(BKx[0:32, 0:128], QM2B[:, 0:32], CID[:])
                ins = t.transpose(BKx[0:32, 128:256], KM32B[:, 0:32], CID[:])
                cT.inc(ins)

            block.tensor(p4_t1)
            c2 = cT.n

            def p4_v2(v, c2=c2):
                _w(v, cT, c2)
                v.reduce_max(out=SM1[0:32, 0:1], in_=BK41[0:32, 0:128], axis=AX.X)
                _ss(v, cV, v.reduce_max(out=SM1[0:32, 1:2], in_=BK41[0:32, 128:256], axis=AX.X))
                ins = v.tensor_mul(out=SM1[0:32, 2:3], in0=SM1[0:32, 0:1], in1=SM1[0:32, 1:2])
                cV.inc(ins)

            block.vector(p4_v2)
            c3 = cV.n
            pt_g["cons"][b41] = [(cV, c3)]

            def p4_s1(s, c3=c3):
                _w(s, cV, c3)
                ins = s.activation(out=SM1[0:32, 3:4], in_=SM1[0:32, 2:3], func=AF.Sqrt,
                                   scale=1.0 / 64.0)
                cS.inc(ins)

            block.scalar(p4_s1)
            c4 = cS.n

            def p4_v3(v, c4=c4):
                _w(v, cS, c4)
                _ss(v, cV, v.tensor_scalar(out=SM1[0:32, 4:5], in0=SM1[0:32, 3:4],
                                           scalar1=-1.0, scalar2=35.0,
                                           op0=ALU.mult, op1=ALU.add))
                ins = v.tensor_copy(out=QM2B[0:32, 0:1], in_=SM1[0:32, 4:5])
                cV.inc(ins)

            block.vector(p4_v3)
            c5 = cV.n

            b42 = pt_g["g"] % 2
            BK42 = PT if b42 == 0 else PT2

            def p4_t2(t, c5=c5):
                _w(t, cV, c5)
                _b, BKx = pt_open(t)
                ins = t.transpose(BKx[0:1, 0:32], QM2B[0:32, 0:1], CID[0:32, 0:32])
                cT.inc(ins)

            block.tensor(p4_t2)
            c6 = cT.n

            def p4_v4(v, c6=c6):
                _w(v, cT, c6)
                ins = v.tensor_copy(out=CAPR[0:1, :], in_=BK42[0:1, 0:32])
                cV.inc(ins)

            block.vector(p4_v4)
            c7 = cV.n
            pt_g["cons"][b42] = [(cV, c7)]

            def p4_t3(t, c7=c7):
                _w(t, cV, c7)
                for c_ in pt_g["cons"][1]:
                    _wc(t, c_)
                ins = t.matmul(PR0[:, 0:32], CONEB[0:1, 0:128], CAPR[0:1, 0:32],
                               start=True, stop=True)
                cT.inc(ins)

            block.tensor(p4_t3)
            c8 = cT.n

            def p4_v5(v, c8=c8):
                _w(v, cT, c8)
                _w(v, dY, CONST_DY)
                _ss(v, cV, v.tensor_copy(out=CAPB[:], in_=PR0[:, 0:32]))
                _ss(v, cV, v.tensor_tensor(out=CAPP[:], in0=CAPB[:, 0:32:2], in1=CAPB[:, 1:32:2],
                                           op=ALU.min))
                ins = v.tensor_tensor(out=CAPZ[:], in0=CAPP[:],
                                      in1=CZF[:, 0:1].broadcast_to([128, 16]), op=ALU.add)
                cV.inc(ins)

            block.vector(p4_v5)
            cap_done = cV.n
            pt_g["cons"][1] = [(cV, cap_done)]

            if debug:
                def pdump1(s):
                    _w(s, cV, cap_done)
                    _w(s, cS, kd_done_s)
                    for dst, src in [(dbgout["h1fm"][:], A64[:, 0:32768]),
                                     (dbgout["qfm"][:], AQ[:, 0:16384]),
                                     (dbgout["kd"][:], AKV[:, 0:16384]),
                                     (dbgout["va"][:], AKV[:, 16384:24704]),
                                     (dbgout["capb"][:], CAPB[:])]:
                        s.dma_start(out=dst, in_=src).then_inc(dY.h, 16)
                        dY.n += 16

                block.sync(pdump1)
                d1 = dY.n

                def pgate1(v, d1=d1):
                    _w(v, dY, d1)
                    ins = v.tensor_copy(out=SM1[:, 0:1], in_=SM1[:, 0:1])
                    cV.inc(ins)

                block.vector(pgate1)
                cap_done = cV.n
            _maybe_stop('p4')

            def visits(gi):
                vs = []
                if gi == 0:
                    for k in range(7):
                        off = ((k + 1) // 2) * 128
                        vs.append((k, off, 512 - off, k % 2 == 0, False))
                    vs.append((15, 0, 512, False, True))
                else:
                    for k in range(9):
                        vs.append((k, 0, 512, k == 8, False))
                    for k in range(9, 15):
                        off = ((k - 7) // 2) * 128
                        vs.append((k, off, 512 - off, k % 2 == 0, False))
                    vs.append((15, 0, 512, False, True))
                return vs

            psab = [PSAB0, PSAB1]
            psab_free = [None, None]
            py_free = [None, None]
            pr_free = [(cV, cap_done), (cV, cap_done)]
            atn_free = [None] * 4
            vseq = [0]
            bst_free = [0]
            linv_free = [0]
            prev_sh = [0]
            prev_rb = [0]
            first_p5 = [True]

            def atn_slot_ap(vidx):
                j = vidx % 4
                if j < 2:
                    return ATN[:, j * 1024:(j + 1) * 1024]
                return SQS[:, (j - 2) * 1024:(j - 1) * 1024]

            for gi in range(2):
                vs = visits(gi)
                nv = len(vs)
                for p in range(16):
                    g = p // 2
                    av_last = 0
                    for vi, (k, off, N, diag, zer) in enumerate(vs):
                        sb = vseq[0] % 2
                        PS = psab[sb]
                        slot = atn_slot_ap(vseq[0])
                        slot_i = vseq[0] % 4
                        vseq[0] += 1

                        def p5_score(t, k=k, off=off, N=N, g=g, p=p, gi=gi, PS=PS, sb=sb):
                            if first_p5[0]:
                                _w(t, cV, cap_done)
                                _w(t, cS, kd_done_s)
                                _w(t, dG, MEMSET_DG)
                                first_p5[0] = False
                            _wc(t, psab_free[sb])
                            qsl = p * 1024 + gi * 512 + off
                            t.matmul(PS[:, off:off + N], akv_kd(g, k * 128, 128, 0, 64),
                                     AQ[0:64, qsl:qsl + N], start=True, stop=True)
                            ins = t.matmul(PS[:, 512 + off:512 + off + N],
                                           akv_kd(g, k * 128, 128, 64, 128),
                                           AQ[64:128, qsl:qsl + N], start=True, stop=True,
                                           tile_position=(64, 0))
                            cT.inc(ins)

                        block.tensor(p5_score)
                        sc_cnt = cT.n

                        def p5_exp(s, off=off, N=N, p=p, PS=PS, slot=slot, slot_i=slot_i,
                                   zer=zer, sc=sc_cnt):
                            _w(s, cT, sc)
                            _wc(s, atn_free[slot_i])
                            bias = (CAPZ if zer else CAPP)[:, p:p + 1]
                            ins = s.activation(
                                out=slot.rearrange("p (b n) -> p b n", b=2)[:, :, 0:N],
                                in_=PS[:].rearrange("p (b n) -> p b n", b=2)[:, :, off:off + N],
                                func=AF.Exp, bias=bias, scale=0.125)
                            cS.inc(ins)

                        block.scalar(p5_exp)
                        psab_free[sb] = (cS, cS.n)
                        exp_cnt = cS.n

                        mask_cnt = None
                        if diag:
                            def p5_mask(v, slot=slot, ec=exp_cnt):
                                _w(v, cS, ec)
                                mv = slot.rearrange("p (b n) -> p b n", b=2)[:, :, 0:128]
                                ins = v.tensor_tensor(
                                    out=mv, in0=mv,
                                    in1=CMSK[:].unsqueeze(1).broadcast_to([128, 2, 128]),
                                    op=ALU.mult)
                                cV.inc(ins)

                            block.vector(p5_mask)
                            mask_cnt = cV.n

                        def p5_av(t, k=k, off=off, N=N, g=g, slot=slot, vi=vi, ec=exp_cnt,
                                  mk=mask_cnt, nv=nv):
                            if mk is not None:
                                _w(t, cV, mk)
                            else:
                                _w(t, cS, ec)
                            if vi == 0:
                                _wc(t, py_free[0])
                                _wc(t, py_free[1])
                            t.matmul(PY0[0:65, off:off + N], akv_va(k, g), slot[:, 0:N],
                                     start=(vi == 0), stop=(vi == nv - 1))
                            ins = t.matmul(PY1[0:65, off:off + N], akv_va(k, g),
                                           slot[:, 512:512 + N],
                                           start=(vi == 0), stop=(vi == nv - 1))
                            cT.inc(ins)

                        block.tensor(p5_av)
                        atn_free[slot_i] = (cT, cT.n)
                        av_last = cT.n

                    def p5_lcopy(v, av=av_last):
                        _w(v, cT, av)
                        _w(v, dS, prev_sh[0])
                        v.tensor_copy(out=LROW[64:65, 0:512], in_=PY0[64:65, 0:512])
                        ins = v.tensor_copy(out=LROW[64:65, 512:1024], in_=PY1[64:65, 0:512])
                        cV.inc(ins)

                    block.vector(p5_lcopy)
                    lc = cV.n

                    def p5_shift(s, lc=lc):
                        _w(s, cV, lc)
                        _w(s, cT, prev_rb[0])
                        s.dma_start(out=LA0, in_=LROW[64:65, 0:512]).then_inc(dS.h, 16)
                        s.dma_start(out=LB0, in_=LROW[64:65, 512:1024]).then_inc(dS.h, 16)
                        dS.n += 32

                    block.scalar(p5_shift)
                    sh = dS.n
                    prev_sh[0] = sh

                    def p5_repl_a(t, sh=sh):
                        _w(t, dS, sh)
                        _wc(t, pr_free[0])
                        ins = t.matmul(PR0[0:64, :], CONEF[0:1, 0:64], LA0, start=True, stop=True)
                        cT.inc(ins)

                    block.tensor(p5_repl_a)
                    ra = cT.n

                    def p5_lninv_a(s, ra=ra):
                        _w(s, cT, ra)
                        _w(s, cV, linv_free[0])
                        _ss(s, cS, s.activation(out=LINV[0:64, :], in_=PR0[0:64, :], func=AF.Ln))
                        ins = s.activation(out=LINV[0:64, :], in_=LINV[0:64, :], func=AF.Exp,
                                           scale=-1.0)
                        cS.inc(ins)

                    block.scalar(p5_lninv_a)
                    la = cS.n
                    pr_free[0] = (cS, la)

                    def p5_evac_a(v, p=p, gi=gi, la=la):
                        _w(v, cS, la)
                        ins = v.scalar_tensor_tensor(out=a64_y(p, gi * 512, 512, 0, 64),
                                                     in0=PY0[0:64, :], scalar=1.0,
                                                     in1=LINV[0:64, :], op0=ALU.mult, op1=ALU.mult)
                        cV.inc(ins)

                    block.vector(p5_evac_a)
                    eva = cV.n
                    py_free[0] = (cV, eva)

                    def p5_repl_b(t, sh=sh):
                        _w(t, dS, sh)
                        _wc(t, pr_free[1])
                        ins = t.matmul(PR1f[0:64, :], CONEF[0:1, 0:64], LB0, start=True, stop=True)
                        cT.inc(ins)

                    block.tensor(p5_repl_b)
                    rb = cT.n
                    prev_rb[0] = rb

                    def p5_lninv_b(s, rb=rb, eva=eva):
                        _w(s, cT, rb)
                        _w(s, cV, eva)
                        _ss(s, cS, s.activation(out=LINV[0:64, :], in_=PR1f[0:64, :], func=AF.Ln))
                        ins = s.activation(out=LINV[0:64, :], in_=LINV[0:64, :], func=AF.Exp,
                                           scale=-1.0)
                        cS.inc(ins)

                    block.scalar(p5_lninv_b)
                    lb = cS.n
                    pr_free[1] = (cS, lb)

                    def p5_evac_b(v, lb=lb):
                        _w(v, cS, lb)
                        _w(v, dG, bst_free[0])
                        ins = v.scalar_tensor_tensor(out=BST[0:64, :], in0=PY1[0:64, :],
                                                     scalar=1.0, in1=LINV[0:64, :],
                                                     op0=ALU.mult, op1=ALU.mult)
                        cV.inc(ins)

                    block.vector(p5_evac_b)
                    evb = cV.n
                    py_free[1] = (cV, evb)
                    linv_free[0] = evb

                    def p5_shift_b(g_, p=p, gi=gi, evb=evb):
                        _w(g_, cV, evb)
                        g_.dma_start(out=a64_y(p, gi * 512, 512, 64, 128), in_=BST[0:64, :]
                                     ).then_inc(dG.h, 16)
                        dG.n += 16

                    block.gpsimd(p5_shift_b)
                    bst_free[0] = dG.n
                if gi == 0:
                    _maybe_stop('p5g0')
            yfm_done_v = cV.n
            yfm_done_g = dG.n
            p5_end_cs = cS.n
            pt_g["cons"][0] = [(cS, p5_end_cs)]
            pt_g["cons"][1] = [(cS, p5_end_cs)]

            if debug:
                def pdump2(s):
                    _w(s, cV, yfm_done_v)
                    _w(s, dG, yfm_done_g)
                    s.dma_start(out=dbgout["yfm"][:], in_=A64[:, 0:16384]).then_inc(dY.h, 16)
                    dY.n += 16

                block.sync(pdump2)
            _maybe_stop('p5')

            wo_free = (cT, p3_end_ct)
            obank_free = [py_free[0], py_free[1]]
            x2_done = [0] * NS
            for dgq in range(4):
                def p6_wo(g, dgq=dgq, wo_free=wo_free):
                    _wc(g, wo_free)
                    g.dma_start(out=WBUF[:, 0:8192].rearrange("p (c j) -> p c j", j=512),
                                in_=wot[:, dgq * 512:(dgq + 1) * 512]
                                .rearrange("(c p) j -> p c j", p=128)).then_inc(dG.h, 16)
                    dG.n += 16

                block.gpsimd(p6_wo)
                wo_ld = dG.n

                for tau in range(NS):
                    bank = PJ[4 + tau % 2]
                    bidx = tau % 2

                    def p6_mm(t, dgq=dgq, tau=tau, bank=bank, bidx=bidx, wo_ld=wo_ld):
                        _w(t, dG, wo_ld)
                        if dgq == 0 and tau == 0:
                            _w(t, cV, yfm_done_v)
                            _w(t, dG, yfm_done_g)
                        _wc(t, obank_free[bidx])
                        for cc in range(16):
                            ins = t.matmul(bank, a64_y(cc, tau * 128, 128),
                                           WBUF[:, cc * 512:(cc + 1) * 512],
                                           start=(cc == 0), stop=(cc == 15))
                        cT.inc(ins)

                    block.tensor(p6_mm)
                    omm = cT.n

                    def p6_ev(v, dgq=dgq, tau=tau, bank=bank, omm=omm):
                        _w(v, cT, omm)
                        if tau >= 6 and dgq == 0:
                            _w(v, dG, yfm_done_g)
                        ins = v.tensor_copy(out=x2loc(tau)[:, dgq * 512:(dgq + 1) * 512], in_=bank)
                        cV.inc(ins)

                    block.vector(p6_ev)
                    obank_free[bidx] = (cV, cV.n)
                    if dgq == 3:
                        xt = x2loc(tau)
                        oe = cV.n

                        def p6_stats(s, xt=xt, oe=oe):
                            _w(s, cV, oe)
                            s.activation(out=SQS[:], in_=xt, func=AF.Square, accum_out=STT[:, 0:1])
                            ins = s.activation(out=SQS[:], in_=xt, func=AF.Copy,
                                               accum_out=STT[:, 1:2])
                            cS.inc(ins)

                        block.scalar(p6_stats)
                        st1 = cS.n

                        def p6_small(v, st1=st1):
                            _w(v, cS, st1)
                            v.tensor_scalar_mul(out=SM1[:, 0:1], in0=STT[:, 1:2], scalar1=1.0 / D)
                            _ss(v, cV, v.tensor_scalar_mul(out=SM1[:, 1:2], in0=STT[:, 0:1],
                                                           scalar1=1.0 / D))
                            _ss(v, cV, v.scalar_tensor_tensor(out=STT[:, 2:3], in0=SM1[:, 0:1],
                                                              scalar=-1.0, in1=SM1[:, 0:1],
                                                              op0=ALU.mult, op1=ALU.mult))
                            ins = v.tensor_add(out=STT[:, 2:3], in0=STT[:, 2:3], in1=SM1[:, 1:2])
                            cV.inc(ins)

                        block.vector(p6_small)
                        sv1 = cV.n

                        def p6_sqrt(s, sv1=sv1):
                            _w(s, cV, sv1)
                            ins = s.activation(out=STT[:, 3:4], in_=STT[:, 2:3], func=AF.Sqrt,
                                               bias=CEPS[:, 0:1])
                            cS.inc(ins)

                        block.scalar(p6_sqrt)
                        st2 = cS.n

                        def p6_norm(v, xt=xt, st2=st2):
                            _w(v, cS, st2)
                            _ss(v, cV, v.reciprocal(out=STT[:, 3:4], in_=STT[:, 3:4]))
                            _ss(v, cV, v.scalar_tensor_tensor(out=STT[:, 4:5], in0=SM1[:, 0:1],
                                                              scalar=-1.0, in1=STT[:, 3:4],
                                                              op0=ALU.mult, op1=ALU.mult))
                            ins = v.tensor_scalar(out=xt, in0=xt, scalar1=STT[:, 3:4],
                                                  scalar2=STT[:, 4:5], op0=ALU.mult, op1=ALU.add)
                            cV.inc(ins)

                        block.vector(p6_norm)
                        nv1 = cV.n

                        def p6_resid(g, xt=xt, tau=tau, nv1=nv1):
                            _w(g, cV, nv1)
                            g.dma_start(out=xt, in_=xkv[2 * tau * 128:(2 * tau + 1) * 128, :],
                                        accum_op=ALU.add).then_inc(dG.h, 16)
                            dG.n += 16

                        block.gpsimd(p6_resid)
                        x2_done[tau] = dG.n
                wo_free = (cT, cT.n)
            p6_end_ct = cT.n
            _maybe_stop('p6')

            h2_done = [None] * NS
            h2t_used = [0] * NS
            for tau in range(NS):
                xt = x2loc(tau)

                def p7_stats(s, xt=xt, tau=tau):
                    _w(s, dG, x2_done[tau])
                    s.activation(out=SQS[:], in_=xt, func=AF.Square, accum_out=STT[:, 0:1])
                    ins = s.activation(out=SQS[:], in_=xt, func=AF.Copy, accum_out=STT[:, 1:2])
                    cS.inc(ins)

                block.scalar(p7_stats)
                st1 = cS.n

                def p7_small(v, st1=st1):
                    _w(v, cS, st1)
                    v.tensor_scalar_mul(out=SM1[:, 0:1], in0=STT[:, 1:2], scalar1=1.0 / D)
                    _ss(v, cV, v.tensor_scalar_mul(out=SM1[:, 1:2], in0=STT[:, 0:1],
                                                   scalar1=1.0 / D))
                    _ss(v, cV, v.scalar_tensor_tensor(out=STT[:, 2:3], in0=SM1[:, 0:1],
                                                      scalar=-1.0, in1=SM1[:, 0:1],
                                                      op0=ALU.mult, op1=ALU.mult))
                    ins = v.tensor_add(out=STT[:, 2:3], in0=STT[:, 2:3], in1=SM1[:, 1:2])
                    cV.inc(ins)

                block.vector(p7_small)
                sv1 = cV.n

                def p7_sqrt(s, sv1=sv1):
                    _w(s, cV, sv1)
                    ins = s.activation(out=STT[:, 3:4], in_=STT[:, 2:3], func=AF.Sqrt,
                                       bias=CEPS[:, 0:1])
                    cS.inc(ins)

                block.scalar(p7_sqrt)
                st2 = cS.n

                def p7_inv(v, st2=st2):
                    _w(v, cS, st2)
                    _ss(v, cV, v.reciprocal(out=STT[:, 3:4], in_=STT[:, 3:4]))
                    ins = v.scalar_tensor_tensor(out=STT[:, 4:5], in0=SM1[:, 0:1], scalar=-1.0,
                                                 in1=STT[:, 3:4], op0=ALU.mult, op1=ALU.mult)
                    cV.inc(ins)

                block.vector(p7_inv)
                iv1 = cV.n

                ht = H1T[:, (tau % 2) * 2048:((tau % 2) + 1) * 2048]

                def p7_norm(s, xt=xt, ht=ht, iv1=iv1, tau=tau):
                    _w(s, cV, iv1)
                    if tau >= 2:
                        _w(s, cT, h2t_used[tau - 2])
                    ins = s.activation(out=ht, in_=xt, func=AF.Identity, scale=STT[:, 3:4],
                                       bias=STT[:, 4:5])
                    cS.inc(ins)

                block.scalar(p7_norm)
                nd = cS.n

                def p7_pre(t, nd=nd):
                    _w(t, cS, nd)

                r = emit_tr16(lambda c, ht=ht: ht[:, c * 128:(c + 1) * 128],
                              lambda c, tau=tau: aq_q(c, tau * 128, 128), p7_pre)
                h2t_used[tau] = r["ct"]
                h2_done[tau] = (r["cv"], r["cs"])

            if debug:
                def pdump3(s):
                    _w(s, cV, h2_done[NS - 1][0])
                    _w(s, cS, h2_done[NS - 1][1])
                    s.dma_start(out=dbgout["h2fm"][:], in_=AQ[:, 0:16384]).then_inc(dY.h, 16)
                    dY.n += 16
                    for tau in range(NS):
                        s.dma_start(out=dbgout["x2"][:, tau * D:(tau + 1) * D], in_=x2loc(tau)
                                    ).then_inc(dY.h, 16)
                        dY.n += 16

                block.sync(pdump3)
                d3 = dY.n

                def pgate3(v, d3=d3):
                    _w(v, dY, d3)
                    ins = v.tensor_copy(out=SM1[:, 0:1], in_=SM1[:, 0:1])
                    cV.inc(ins)

                block.vector(pgate3)
                h2_done = [(cV.n, h2_done[t_][1]) for t_ in range(NS)]
            _maybe_stop('p7')

            dn_mm_last = 0
            hid_free_cnt = 0
            gu_free = [[(cT, p6_end_ct)] * 2 for _ in range(2)]
            gate_pfree = [psab_free[0], psab_free[0]]
            up_pfree = [psab_free[1], psab_free[1]]
            gslot_free = [atn_free[2], atn_free[3]]
            uslot_free = [atn_free[0], atn_free[1]]
            d_bank_free = [obank_free[0], obank_free[1]]
            ffb_free = None
            outtile_free = None
            hid_written = [0] * NHC
            out_dY = 0
            for tg in range(2):
                h2v = max(h2_done[4 * tg + j][0] for j in range(4))
                h2s = max(h2_done[4 * tg + j][1] for j in range(4))
                if tg == 1:
                    gu_free = [[(cT, dn_mm_last)] * 2 for _ in range(2)]
                gu_mm = [0] * NHC
                for hc in range(NHC):
                    slot = hc % 2

                    def p8_wload(g, hc=hc, slot=slot):
                        for mi, (mat, base) in enumerate([(gt, 0), (ut, 4096)]):
                            _wc(g, gu_free[mi][slot])
                            src = mat[:, hc * 128:(hc + 1) * 128].rearrange("(c p) h -> p c h",
                                                                            p=128)
                            g.dma_start(out=WBUF[:, base + slot * 2048: base + (slot + 1) * 2048]
                                        .rearrange("p (c h) -> p c h", h=128), in_=src
                                        ).then_inc(dG.h, 16)
                            dG.n += 16

                    block.gpsimd(p8_wload)
                    wld = dG.n

                    if hc % 2 == 0:
                        def p8_sub_v(v, slot=slot, wld=wld):
                            _w(v, dG, wld)
                            v.tensor_scalar_add(out=WBUF[:, slot * 2048:(slot + 1) * 2048],
                                                in0=WBUF[:, slot * 2048:(slot + 1) * 2048],
                                                scalar1=-128.0)
                            ins = v.tensor_scalar_add(
                                out=WBUF[:, 4096 + slot * 2048:4096 + (slot + 1) * 2048],
                                in0=WBUF[:, 4096 + slot * 2048:4096 + (slot + 1) * 2048],
                                scalar1=-128.0)
                            cV.inc(ins)
                        block.vector(p8_sub_v)
                        sub = (cV, cV.n)
                    else:
                        def p8_sub_s(s, slot=slot, wld=wld):
                            _w(s, dG, wld)
                            s.activation(out=WBUF[:, slot * 2048:(slot + 1) * 2048],
                                         in_=WBUF[:, slot * 2048:(slot + 1) * 2048],
                                         func=AF.Identity, bias=CM128[:, 0:1])
                            ins = s.activation(
                                out=WBUF[:, 4096 + slot * 2048:4096 + (slot + 1) * 2048],
                                in_=WBUF[:, 4096 + slot * 2048:4096 + (slot + 1) * 2048],
                                func=AF.Identity, bias=CM128[:, 0:1])
                            cS.inc(ins)
                        block.scalar(p8_sub_s)
                        sub = (cS, cS.n)

                    pg = PJ[0 + slot]
                    pu = PJ[2 + slot]

                    def p8_gu_mm(t, tg=tg, hc=hc, slot=slot, pg=pg, pu=pu, sub=sub,
                                 h2v=h2v, h2s=h2s):
                        _wc(t, sub)
                        if hc == 0:
                            _w(t, cV, h2v)
                            _w(t, cS, h2s)
                        _wc(t, gate_pfree[slot])
                        _wc(t, up_pfree[slot])
                        for cc in range(16):
                            t.matmul(pg,
                                     WBUF[:, slot * 2048 + cc * 128: slot * 2048 + (cc + 1) * 128],
                                     aq_q(cc, tg * 512, 512), start=(cc == 0), stop=(cc == 15))
                        for cc in range(16):
                            ins = t.matmul(pu, WBUF[:, 4096 + slot * 2048 + cc * 128:
                                                    4096 + slot * 2048 + (cc + 1) * 128],
                                           aq_q(cc, tg * 512, 512), start=(cc == 0),
                                           stop=(cc == 15))
                        cT.inc(ins)

                    block.tensor(p8_gu_mm)
                    gu_mm[hc] = cT.n
                    gu_free[0][slot] = (cT, cT.n)
                    gu_free[1][slot] = (cT, cT.n)

                    gslot = SQS[:, slot * 512:(slot + 1) * 512]
                    uslot = ATN[:, slot * 512:(slot + 1) * 512]

                    def p8_gu_ev(s, tg=tg, hc=hc, slot=slot, pg=pg, pu=pu, gslot=gslot,
                                 uslot=uslot, mc=gu_mm[hc]):
                        _w(s, cT, mc)
                        if tg == 0 and hc == 0:
                            _w(s, dY, CONST_DY)
                            _w(s, cS, p5_end_cs)
                        _wc(s, gslot_free[slot])
                        _wc(s, uslot_free[slot])
                        s.activation(out=gslot, in_=pg, func=AF.Silu, scale=CSG[:, hc:hc + 1])
                        ins = s.activation(out=uslot, in_=pu, func=AF.Copy,
                                           scale=CSG[:, 44 + hc:45 + hc])
                        cS.inc(ins)

                    block.scalar(p8_gu_ev)
                    gate_pfree[slot] = (cS, cS.n)
                    up_pfree[slot] = (cS, cS.n)
                    ev_cnt = cS.n

                    def p8_hid(v, tg=tg, hc=hc, gslot=gslot, uslot=uslot, ec=ev_cnt,
                               hfc=hid_free_cnt):
                        _w(v, cS, ec)
                        if tg == 1:
                            _w(v, cT, hfc)
                        ins = v.tensor_tensor(out=a64_hid(hc, 0, 512), in0=gslot, in1=uslot,
                                              op=ALU.mult)
                        cV.inc(ins)

                    block.vector(p8_hid)
                    gslot_free[slot] = (cV, cV.n)
                    uslot_free[slot] = (cV, cV.n)
                    hid_written[hc] = cV.n
                gu_mm_last = gu_mm[NHC - 1]

                dn_free = [(cT, gu_mm_last), (cT, gu_mm_last)]
                for dc in range(NDC):
                    def p8_dload(g, tg=tg, dc=dc, dn_free=dn_free):
                        for half in range(2):
                            _wc(g, dn_free[half])
                            src = dnt[half * 2816:(half + 1) * 2816, dc * 128:(dc + 1) * 128
                                      ].rearrange("(c p) d -> p c d", p=128)
                            g.dma_start(out=WBUF[:, half * 2816:(half + 1) * 2816]
                                        .rearrange("p (c d) -> p c d", d=128), in_=src
                                        ).then_inc(dG.h, 16)
                            dG.n += 16

                    block.gpsimd(p8_dload)
                    dld = dG.n

                    if dc % 2 == 0:
                        def p8_dsub_v(v, dld=dld):
                            _w(v, dG, dld)
                            ins = v.tensor_scalar_add(out=WBUF[:, 0:5632], in0=WBUF[:, 0:5632],
                                                      scalar1=-128.0)
                            cV.inc(ins)
                        block.vector(p8_dsub_v)
                        dsub = (cV, cV.n)
                    else:
                        def p8_dsub_s(s, dld=dld):
                            _w(s, dG, dld)
                            ins = s.activation(out=WBUF[:, 0:5632], in_=WBUF[:, 0:5632],
                                               func=AF.Identity, bias=CM128[:, 0:1])
                            cS.inc(ins)
                        block.scalar(p8_dsub_s)
                        dsub = (cS, cS.n)

                    pd = PJ[4 + dc % 2]

                    def p8_dmm(t, tg=tg, dc=dc, pd=pd, dsub=dsub):
                        _wc(t, dsub)
                        if dc == 0:
                            _w(t, cV, hid_written[NHC - 1])
                        _wc(t, d_bank_free[dc % 2])
                        for hc in range(NHC):
                            ins = t.matmul(pd, WBUF[:, hc * 128:(hc + 1) * 128],
                                           a64_hid(hc, 0, 512), start=(hc == 0),
                                           stop=(hc == NHC - 1))
                        cT.inc(ins)

                    block.tensor(p8_dmm)
                    dmm = cT.n
                    dn_free[0] = (cT, dmm)
                    dn_free[1] = (cT, dmm)

                    def p8_dev(s, dc=dc, pd=pd, dmm=dmm, ffb_free=ffb_free):
                        _w(s, cT, dmm)
                        _wc(s, ffb_free)
                        ins = s.activation(out=a64_ffb(dc, 0, 512), in_=pd, func=AF.Copy,
                                           scale=CSD[:, dc:dc + 1])
                        cS.inc(ins)

                    block.scalar(p8_dev)
                    d_bank_free[dc % 2] = (cS, cS.n)
                dn_mm_last = cT.n
                hid_free_cnt = cT.n
                ffb_done = cS.n

                for j in range(4):
                    tau = 4 * tg + j

                    def p8_pre(t, fd=ffb_done, outtile_free=outtile_free):
                        _w(t, cS, fd)
                        _wc(t, outtile_free)

                    r = emit_tr16(lambda c_, j=j: a64_ffb(c_, j * 128, 128),
                                  lambda c_: OUTTILE[:, c_ * 128:(c_ + 1) * 128], p8_pre)
                    trv = r["cv"]
                    trs = r["cs"]

                    def p8_l4stats(s, trv=trv, trs=trs):
                        _w(s, cV, trv)
                        _w(s, cS, trs)
                        s.activation(out=SQS[:], in_=OUTTILE[:], func=AF.Square,
                                     accum_out=STT[:, 0:1])
                        ins = s.activation(out=SQS[:], in_=OUTTILE[:], func=AF.Copy,
                                           accum_out=STT[:, 1:2])
                        cS.inc(ins)

                    block.scalar(p8_l4stats)
                    st1 = cS.n

                    def p8_l4small(v, st1=st1):
                        _w(v, cS, st1)
                        v.tensor_scalar_mul(out=SM1[:, 0:1], in0=STT[:, 1:2], scalar1=1.0 / D)
                        _ss(v, cV, v.tensor_scalar_mul(out=SM1[:, 1:2], in0=STT[:, 0:1],
                                                       scalar1=1.0 / D))
                        _ss(v, cV, v.scalar_tensor_tensor(out=STT[:, 2:3], in0=SM1[:, 0:1],
                                                          scalar=-1.0, in1=SM1[:, 0:1],
                                                          op0=ALU.mult, op1=ALU.mult))
                        ins = v.tensor_add(out=STT[:, 2:3], in0=STT[:, 2:3], in1=SM1[:, 1:2])
                        cV.inc(ins)

                    block.vector(p8_l4small)
                    sv1 = cV.n

                    def p8_l4sqrt(s, sv1=sv1):
                        _w(s, cV, sv1)
                        ins = s.activation(out=STT[:, 3:4], in_=STT[:, 2:3], func=AF.Sqrt,
                                           bias=CEPS[:, 0:1])
                        cS.inc(ins)

                    block.scalar(p8_l4sqrt)
                    st2 = cS.n

                    def p8_l4norm(v, st2=st2):
                        _w(v, cS, st2)
                        _ss(v, cV, v.reciprocal(out=STT[:, 3:4], in_=STT[:, 3:4]))
                        _ss(v, cV, v.scalar_tensor_tensor(out=STT[:, 4:5], in0=SM1[:, 0:1],
                                                          scalar=-1.0, in1=STT[:, 3:4],
                                                          op0=ALU.mult, op1=ALU.mult))
                        ins = v.tensor_scalar(out=OUTTILE[:], in0=OUTTILE[:], scalar1=STT[:, 3:4],
                                              scalar2=STT[:, 4:5], op0=ALU.mult, op1=ALU.add)
                        cV.inc(ins)

                    block.vector(p8_l4norm)
                    nv1 = cV.n

                    def p8_resid(g, tau=tau, nv1=nv1):
                        _w(g, cV, nv1)
                        g.dma_start(out=OUTTILE[:], in_=x2loc(tau), accum_op=ALU.add
                                    ).then_inc(dG.h, 16)
                        dG.n += 16

                    block.gpsimd(p8_resid)
                    rg = dG.n

                    def p8_store(s, tau=tau, rg=rg):
                        _w(s, dG, rg)
                        s.dma_start(out=yout[tau * 128:(tau + 1) * 128, :], in_=OUTTILE[:]
                                    ).then_inc(dY.h, 16)
                        dY.n += 16

                    block.sync(p8_store)
                    outtile_free = (dY, dY.n)
                    out_dY = dY.n
                ffb_free = (cT, cT.n)
        except _StopEmit:
            out_dY = dY.n

        def p_final(s, od=out_dY):
            _w(s, dY, od)

        block.sync(p_final)

        real_block.sync(lambda e: _replay(e, RECS["sync"]))
        real_block.gpsimd(lambda e: _replay(e, RECS["gpsimd"]))
        real_block.tensor(lambda e: _replay(e, RECS["tensor"]))
        real_block.vector(lambda e: _replay(e, RECS["vector"]))
        real_block.scalar(lambda e: _replay(e, RECS["scalar"]))

    return nc


def _prep(L0_attn_x, L0_attn_wq, L0_attn_wk, L0_attn_wv, L0_attn_wo,
          L0_ffn_wq_gate, L0_ffn_s_gate, L0_ffn_wq_up, L0_ffn_s_up,
          L0_ffn_wq_down, L0_ffn_s_down):
    x = np.asarray(L0_attn_x, dtype=np.float32)
    shared = {
        "wqt": np.ascontiguousarray(np.asarray(L0_attn_wq, np.float32).T),
        "wkt": np.ascontiguousarray(np.asarray(L0_attn_wk, np.float32).T),
        "wvt": np.ascontiguousarray(np.asarray(L0_attn_wv, np.float32).T),
        "wot": np.ascontiguousarray(np.asarray(L0_attn_wo, np.float32).T),
        "gt": np.ascontiguousarray(np.asarray(L0_ffn_wq_gate, np.int32).T),
        "ut": np.ascontiguousarray(np.asarray(L0_ffn_wq_up, np.int32).T),
        "dnt": np.ascontiguousarray(np.asarray(L0_ffn_wq_down, np.int32).T),
        "sgup": np.ascontiguousarray(np.concatenate(
            [np.asarray(L0_ffn_s_gate, np.float32).reshape(NHC, 128).T,
             np.asarray(L0_ffn_s_up, np.float32).reshape(NHC, 128).T], axis=1)),
        "sdn": np.ascontiguousarray(np.asarray(L0_ffn_s_down, np.float32).reshape(16, 128).T),
        "idn": np.eye(128).astype(BF),
    }
    kk, qq = np.meshgrid(np.arange(128), np.arange(128), indexing="ij")
    shared["dmask"] = (qq >= kk).astype(np.float32).astype(BF)
    theta = 1.0 / (10000.0 ** (np.arange(0, HD, 2, dtype=np.float32) / HD))
    freqs = np.arange(T, dtype=np.float32)[:, None] * theta[None, :]
    cosf = np.cos(freqs).astype(np.float32)
    sinf = np.sin(freqs).astype(np.float32)

    in_maps = []
    for core in range(8):
        seq, role = core // 2, core % 2
        sh = 128 * role
        m = dict(shared)
        m["xkv"] = np.ascontiguousarray(np.roll(x[seq], -sh, axis=0)) if sh else \
            np.ascontiguousarray(x[seq])
        m["cost"] = np.ascontiguousarray(np.roll(cosf, -sh, axis=0)) if sh else cosf
        m["sint"] = np.ascontiguousarray(np.roll(sinf, -sh, axis=0)) if sh else sinf
        m["zflag"] = np.full((128, 1), -50000.0 if role == 0 else 0.0, dtype=np.float32)
        in_maps.append(m)
    return in_maps


def kernel(L0_attn_x, L0_attn_wq, L0_attn_wk, L0_attn_wv, L0_attn_wo,
           L0_ffn_wq_gate, L0_ffn_s_gate, L0_ffn_wq_up, L0_ffn_s_up,
           L0_ffn_wq_down, L0_ffn_s_down):
    in_maps = _prep(L0_attn_x, L0_attn_wq, L0_attn_wk, L0_attn_wv, L0_attn_wo,
                    L0_ffn_wq_gate, L0_ffn_s_gate, L0_ffn_wq_up, L0_ffn_s_up,
                    L0_ffn_wq_down, L0_ffn_s_down)
    nc = build_nc()
    res = run_bass_kernel_spmd(nc, in_maps, core_ids=list(range(8)))
    out = np.empty((4, T, D), dtype=np.float32)
    for core in range(8):
        seq, role = core // 2, core % 2
        yc = np.asarray(res.results[core]["yout"])
        for tau in range(NS):
            jg = 2 * tau + role
            out[seq, jg * 128:(jg + 1) * 128, :] = yc[tau * 128:(tau + 1) * 128, :]
    return out


# version 03
# speedup vs baseline: 1.0132x; 1.0132x over previous
import numpy as np
import ml_dtypes
import concourse.bass as bass
import concourse.mybir as mybir
from concourse.bass_utils import run_bass_kernel_spmd

F32 = mybir.dt.float32
BF16 = mybir.dt.bfloat16
I32 = mybir.dt.int32
AF = mybir.ActivationFunctionType
ALU = mybir.AluOpType
AX = mybir.AxisListType
BF = ml_dtypes.bfloat16

D = 2048
T = 2048
HD = 64
HID = 5632
NT = 16
NS = 8
NHC = HID // 128
NDC = D // 128


class Ctr:
    def __init__(self, handle):
        self.h = handle
        self.n = 0

    def inc(self, ins, k=1):
        ins.then_inc(self.h, k)
        self.n += k
        return self.n


class _RecIns:
    __slots__ = ("incs",)

    def __init__(self):
        self.incs = []

    def then_inc(self, sem, k):
        self.incs.append((sem, k))
        return self


class _Rec:

    def __init__(self):
        self.ops = []

    def __getattr__(self, name):
        def call(*a, **k):
            h = _RecIns()
            self.ops.append((name, a, k, h))
            return h
        return call


def _replay(eng, rec):
    for name, a, k, h in rec.ops:
        ins = getattr(eng, name)(*a, **k)
        for sem, kk in h.incs:
            ins.then_inc(sem, kk)


def _w(eng, ctr, val):
    if ctr is not None and val > 0:
        eng.wait_ge(ctr.h, val)


def _wc(eng, c):
    if c is not None and c[0] is not None and c[1] > 0:
        eng.wait_ge(c[0].h, c[1])


class _StopEmit(Exception):
    pass


def build_nc(debug=False, stop_after=None):
    nc = bass.Bass("TRN2", target_bir_lowering=False)

    def din(name, shape, dt):
        return nc.dram_tensor(name, shape, dt, kind="ExternalInput").ap()

    xkv = din("xkv", [T, D], F32)
    wqt = din("wqt", [D, D], F32)
    wkt = din("wkt", [D, 512], F32)
    wvt = din("wvt", [D, 512], F32)
    wot = din("wot", [D, D], F32)
    gt = din("gt", [D, HID], I32)
    ut = din("ut", [D, HID], I32)
    dnt = din("dnt", [HID, D], I32)
    sgup = din("sgup", [128, 88], F32)
    sdn = din("sdn", [128, 16], F32)
    cost = din("cost", [T, 32], F32)
    sint = din("sint", [T, 32], F32)
    idn = din("idn", [128, 128], BF16)
    dmask = din("dmask", [128, 128], BF16)
    zflag = din("zflag", [128, 1], F32)
    yout = nc.dram_tensor("yout", [NS * 128, D], F32, kind="ExternalOutput").ap()
    dbgout = {}
    if debug:
        dbgout["h1fm"] = nc.dram_tensor("dbg_h1fm", [128, 32768], BF16, kind="ExternalOutput").ap()
        dbgout["kd"] = nc.dram_tensor("dbg_kd", [128, 16384], BF16, kind="ExternalOutput").ap()
        dbgout["va"] = nc.dram_tensor("dbg_va", [128, 8320], BF16, kind="ExternalOutput").ap()
        dbgout["qfm"] = nc.dram_tensor("dbg_qfm", [128, 16384], BF16, kind="ExternalOutput").ap()
        dbgout["capb"] = nc.dram_tensor("dbg_capb", [128, 32], F32, kind="ExternalOutput").ap()
        dbgout["yfm"] = nc.dram_tensor("dbg_yfm", [128, 16384], BF16, kind="ExternalOutput").ap()
        dbgout["x2"] = nc.dram_tensor("dbg_x2", [128, NS * D], F32, kind="ExternalOutput").ap()
        dbgout["h2fm"] = nc.dram_tensor("dbg_h2fm", [128, 16384], BF16, kind="ExternalOutput").ap()

    from contextlib import ExitStack
    with ExitStack() as _st:
        A64 = _st.enter_context(nc.sbuf_tensor("A64", [128, 32768], BF16))
        AKV = _st.enter_context(nc.sbuf_tensor("AKV", [128, 24704], BF16))
        AQ = _st.enter_context(nc.sbuf_tensor("AQ", [128, 16384], BF16))
        WBUF = _st.enter_context(nc.sbuf_tensor("WBUF", [128, 8192], BF16))
        SCR = _st.enter_context(nc.sbuf_tensor("SCR", [128, 4096], F32))
        ATN = _st.enter_context(nc.sbuf_tensor("ATN", [128, 2048], BF16))
        SQS = _st.enter_context(nc.sbuf_tensor("SQS", [128, 2048], BF16))
        H1T = _st.enter_context(nc.sbuf_tensor("H1T", [128, 4096], BF16))
        CID = _st.enter_context(nc.sbuf_tensor("CID", [128, 128], BF16))
        CMSK = _st.enter_context(nc.sbuf_tensor("CMSK", [128, 128], BF16))
        CONEB = _st.enter_context(nc.sbuf_tensor("CONEB", [128, 128], BF16))
        CONEF = _st.enter_context(nc.sbuf_tensor("CONEF", [128, 64], F32))
        CCOS = _st.enter_context(nc.sbuf_tensor("CCOS", [128, 512], F32))
        CSIN = _st.enter_context(nc.sbuf_tensor("CSIN", [128, 512], F32))
        CSG = _st.enter_context(nc.sbuf_tensor("CSG", [128, 88], F32))
        CSD = _st.enter_context(nc.sbuf_tensor("CSD", [128, 16], F32))
        CZF = _st.enter_context(nc.sbuf_tensor("CZF", [128, 1], F32))
        CEPS = _st.enter_context(nc.sbuf_tensor("CEPS", [128, 1], F32))
        CM128 = _st.enter_context(nc.sbuf_tensor("CM128", [128, 1], F32))
        STT = _st.enter_context(nc.sbuf_tensor("STT", [128, 32], F32))
        SM1 = _st.enter_context(nc.sbuf_tensor("SM1", [128, 8], F32))
        QM2 = _st.enter_context(nc.sbuf_tensor("QM2", [128, 32], F32))
        KM2 = _st.enter_context(nc.sbuf_tensor("KM2", [128, 8], F32))
        QM2B = _st.enter_context(nc.sbuf_tensor("QM2B", [128, 32], BF16))
        KM32B = _st.enter_context(nc.sbuf_tensor("KM32B", [128, 32], BF16))
        CAPR = _st.enter_context(nc.sbuf_tensor("CAPR", [1, 32], BF16))
        CAPB = _st.enter_context(nc.sbuf_tensor("CAPB", [128, 32], F32))
        CAPP = _st.enter_context(nc.sbuf_tensor("CAPP", [128, 16], F32))
        CAPZ = _st.enter_context(nc.sbuf_tensor("CAPZ", [128, 16], F32))
        LROW = _st.enter_context(nc.sbuf_tensor("LROW", [128, 1024], F32))
        PSAB0 = _st.enter_context(nc.psum_tensor("PSAB0", [128, 1024], F32))
        PSAB1 = _st.enter_context(nc.psum_tensor("PSAB1", [128, 1024], F32))
        PY0 = _st.enter_context(nc.psum_tensor("PY0", [128, 512], F32))
        PY1 = _st.enter_context(nc.psum_tensor("PY1", [128, 512], F32))
        PR0 = _st.enter_context(nc.psum_tensor("PR0", [128, 512], F32))
        PT = _st.enter_context(nc.psum_tensor("PT", [128, 1024], BF16))
        _dY = _st.enter_context(nc.semaphore("dY"))
        _dG = _st.enter_context(nc.semaphore("dG"))
        _dS = _st.enter_context(nc.semaphore("dS"))
        _cT = _st.enter_context(nc.semaphore("cT"))
        _cV = _st.enter_context(nc.semaphore("cV"))
        _cS = _st.enter_context(nc.semaphore("cS"))
        _cG = _st.enter_context(nc.semaphore("cG"))
        _dX0 = _st.enter_context(nc.semaphore("dX0"))
        _dX1 = _st.enter_context(nc.semaphore("dX1"))
        _dW2 = _st.enter_context(nc.semaphore("dW2"))
        _dB = _st.enter_context(nc.semaphore("dB"))
        _dR = _st.enter_context(nc.semaphore("dR"))
        _dO = _st.enter_context(nc.semaphore("dO"))
        real_block = _st.enter_context(nc.Block())

        dY, dG, dS = Ctr(_dY), Ctr(_dG), Ctr(_dS)
        cT, cV, cS = Ctr(_cT), Ctr(_cV), Ctr(_cS)
        cG = Ctr(_cG)
        dX = [Ctr(_dX0), Ctr(_dX1)]
        dW2 = Ctr(_dW2)
        dB = Ctr(_dB)
        dR = Ctr(_dR)
        dO = Ctr(_dO)

        RECS = {n: _Rec() for n in ("sync", "gpsimd", "tensor", "vector", "scalar")}

        class _Shim:
            def __getattr__(self, name):
                rec = RECS[name]

                def run(fn):
                    fn(rec)
                return run

        block = _Shim()

        def _ss(eng, ctr, ins):
            c = ctr.inc(ins)
            eng.wait_ge(ctr.h, c)

        LA0 = SCR[0:1, 0:512]
        LB0 = SCR[0:1, 512:1024]
        LINV = SCR[:, 1024:1536]
        BST = SCR[:, 1536:1792].bitcast(BF16)

        PR1f = PT[:].bitcast(F32)
        PT2 = PR0[:].bitcast(BF16)

        PJ = [PSAB0[:, 0:512], PSAB0[:, 512:1024], PSAB1[:, 0:512], PSAB1[:, 512:1024],
              PY0[:], PY1[:], PR0[:], PR1f]

        pt_g = {"g": 0, "cons": [[], []]}

        def pt_open(t):
            b = pt_g["g"] % 2
            pt_g["g"] += 1
            for c in pt_g["cons"][b]:
                _wc(t, c)
            pt_g["cons"][b] = []
            return b, (PT if b == 0 else PT2)

        def a64_h1(c, t0, n):
            return A64[:, c * T + t0: c * T + t0 + n]

        def a64_y(c, t0, n, r0=0, r1=128):
            return A64[r0:r1, c * 1024 + t0: c * 1024 + t0 + n]

        def a64_hid(hc, t0, n):
            return A64[:, hc * 512 + t0: hc * 512 + t0 + n]

        def a64_ffb(dc, j0, n):
            return A64[:, 22528 + dc * 512 + j0: 22528 + dc * 512 + j0 + n]

        def akv_kd(g, k0, n, r0=0, r1=128):
            return AKV[r0:r1, g * 2048 + k0: g * 2048 + k0 + n]

        def akv_va(i, g):
            o = 16384 + i * 520 + g * 65
            return AKV[:, o: o + 65]

        AKVf = AKV[:].bitcast(F32)

        def x2loc(tau):
            if tau < 6:
                return AKVf[:, tau * 2048: (tau + 1) * 2048]
            return SCR[:, (tau - 6) * 2048: (tau - 5) * 2048]

        def aq_q(c, t0, n, r0=0, r1=128):
            return AQ[r0:r1, c * 1024 + t0: c * 1024 + t0 + n]

        OUTTILE = H1T[:].bitcast(F32)

        def _maybe_stop(tag):
            if stop_after == tag:
                def pj(s):
                    _w(s, cT, cT.n)
                    _w(s, cV, cV.n)
                    _w(s, cS, cS.n)
                    _w(s, dY, dY.n)
                    _w(s, dG, dG.n)
                    _w(s, dS, dS.n)
                    if debug:
                        for dst, src in [(dbgout["h1fm"][:], A64[:, 0:32768]),
                                         (dbgout["kd"][:], AKV[:, 0:16384]),
                                         (dbgout["va"][:], AKV[:, 16384:24704]),
                                         (dbgout["qfm"][:], AQ[:, 0:16384]),
                                         (dbgout["capb"][:], CAPB[:])]:
                            s.dma_start(out=dst, in_=src).then_inc(dY.h, 16)
                            dY.n += 16
                    s.wait_ge(dY.h, dY.n)

                block.sync(pj)
                raise _StopEmit()

        def emit_tr16(src_fn, dst_fn, pre_wait=None):
            res = {}
            for half in range(2):
                bpeek = pt_g["g"] % 2
                BK = PT if bpeek == 0 else PT2

                def p_tr(t, half=half):
                    if pre_wait is not None and half == 0:
                        pre_wait(t)
                    _b, BKx = pt_open(t)
                    for c8 in range(8):
                        c = half * 8 + c8
                        ins = t.transpose(BKx[:, c8 * 128:(c8 + 1) * 128], src_fn(c), CID[:])
                        cT.inc(ins)

                block.tensor(p_tr)
                close = cT.n

                if half == 0:
                    def p_ev_v(v, half=half, BK=BK, close=close):
                        _w(v, cT, close)
                        for c8 in range(8):
                            ins = v.tensor_copy(out=dst_fn(half * 8 + c8),
                                                in_=BK[:, c8 * 128:(c8 + 1) * 128])
                            cV.inc(ins)

                    block.vector(p_ev_v)
                    pt_g["cons"][bpeek] = [(cV, cV.n)]
                else:
                    def p_ev_s(s, half=half, BK=BK, close=close):
                        _w(s, cT, close)
                        for c8 in range(8):
                            ins = s.copy(out=dst_fn(half * 8 + c8),
                                         in_=BK[:, c8 * 128:(c8 + 1) * 128])
                            cS.inc(ins)

                    block.scalar(p_ev_s)
                    pt_g["cons"][bpeek] = [(cS, cS.n)]
            res["ct"] = cT.n
            res["cv"] = cV.n
            res["cs"] = cS.n
            return res

        rope_tmps = [LROW[:, 256 * j: 256 * (j + 1)].rearrange("p (h j) -> p h j", j=32)
                     for j in range(4)]

        def emit_rope(v, bank, out_bf, i):
            P3v = bank.rearrange("p (h j two) -> p h j two", j=32, two=2)
            Cb = CCOS[:, i * 32:(i + 1) * 32].unsqueeze(1).broadcast_to([128, 8, 32])
            Sb = CSIN[:, i * 32:(i + 1) * 32].unsqueeze(1).broadcast_to([128, 8, 32])
            T1, T2, T3, T4 = rope_tmps
            A3 = out_bf.rearrange("p (h j two) -> p h j two", j=32, two=2)
            v.tensor_tensor(out=T1, in0=P3v[:, :, :, 0], in1=Cb, op=ALU.mult)
            v.tensor_tensor(out=T2, in0=P3v[:, :, :, 1], in1=Cb, op=ALU.mult)
            v.tensor_tensor(out=T3, in0=P3v[:, :, :, 0], in1=Sb, op=ALU.mult)
            _ss(v, cV, v.tensor_tensor(out=T4, in0=P3v[:, :, :, 1], in1=Sb, op=ALU.mult))
            v.tensor_sub(out=A3[:, :, :, 0], in0=T1, in1=T4)
            return v.tensor_add(out=A3[:, :, :, 1], in0=T3, in1=T2)

        NSQ = ATN[:, 1024:1536]

        try:
            def p0_sync(s):
                for dst, src in [(CID[:], idn[:]), (CMSK[:], dmask[:]), (CSG[:], sgup[:]),
                                 (CSD[:], sdn[:]), (CZF[:], zflag[:])]:
                    s.dma_start(out=dst, in_=src).then_inc(dY.h, 16)
                    dY.n += 16
                s.dma_start(out=CCOS[:].rearrange("p (i j) -> p i j", j=32),
                            in_=cost.rearrange("(i p) j -> p i j", p=128)).then_inc(dY.h, 16)
                s.dma_start(out=CSIN[:].rearrange("p (i j) -> p i j", j=32),
                            in_=sint.rearrange("(i p) j -> p i j", p=128)).then_inc(dY.h, 16)
                dY.n += 32

            block.sync(p0_sync)
            CONST_DY = dY.n

            def p0_g(g):
                g.memset(CONEB[:], 1.0)
                g.memset(CONEF[:], 1.0)
                g.memset(CEPS[:], 1e-5)
                g.memset(CM128[:], -128.0)
                ap = AKV[:, 16384:24704].rearrange("p (i r) -> p i r", r=520)[:, :, 64:520:65]
                ins = g.memset(ap, 1.0)
                ins.then_inc(cG.h, 1)
                cG.n += 1

            block.gpsimd(p0_g)
            MEMSET_CG = cG.n
            _maybe_stop('p0')

            h1_done = [None] * NT
            h1t_used = [0] * NT
            xslot_free = [0, 0]
            norm_done = [0] * NT

            for i in range(NT):
                slot = i % 2
                xt = SCR[:, slot * 2048:(slot + 1) * 2048]
                c_ss = STT[:, 4 * slot + 0: 4 * slot + 1]
                c_sx = STT[:, 4 * slot + 1: 4 * slot + 2]
                c_iv = STT[:, 4 * slot + 2: 4 * slot + 3]
                c_nb = STT[:, 4 * slot + 3: 4 * slot + 4]

                def p1_load(s, i=i, xt=xt, slot=slot):
                    _w(s, cS, xslot_free[slot])
                    s.dma_start(out=xt, in_=xkv[i * 128:(i + 1) * 128, :]).then_inc(dX[slot].h, 16)
                    dX[slot].n += 16

                block.sync(p1_load)
                ld = dX[slot].n

                def p1_stats(s, xt=xt, c_ss=c_ss, c_sx=c_sx, ld=ld, i=i, slot=slot):
                    _w(s, dX[slot], ld)
                    if i == 0:
                        _w(s, cG, MEMSET_CG)
                    s.activation(out=SQS[:], in_=xt, func=AF.Square, accum_out=c_ss)
                    ins = s.activation(out=SQS[:], in_=xt, func=AF.Copy, accum_out=c_sx)
                    cS.inc(ins)

                block.scalar(p1_stats)
                stc = cS.n

                def p1_small(v, c_ss=c_ss, c_sx=c_sx, slot=slot, stc=stc):
                    _w(v, cS, stc)
                    m = SM1[:, slot:slot + 1]
                    t1 = SM1[:, 2 + slot:3 + slot]
                    v.tensor_scalar_mul(out=m, in0=c_sx, scalar1=1.0 / D)
                    _ss(v, cV, v.tensor_scalar_mul(out=t1, in0=c_ss, scalar1=1.0 / D))
                    _ss(v, cV, v.scalar_tensor_tensor(out=c_ss, in0=m, scalar=-1.0, in1=m,
                                                      op0=ALU.mult, op1=ALU.mult))
                    ins = v.tensor_add(out=c_ss, in0=c_ss, in1=t1)
                    cV.inc(ins)

                block.vector(p1_small)
                smc = cV.n

                def p1_sqrt(s, c_ss=c_ss, c_iv=c_iv, smc=smc):
                    _w(s, cV, smc)
                    ins = s.activation(out=c_iv, in_=c_ss, func=AF.Sqrt, bias=CEPS[:, 0:1])
                    cS.inc(ins)

                block.scalar(p1_sqrt)
                sqc = cS.n

                def p1_inv(v, c_iv=c_iv, c_nb=c_nb, slot=slot, sqc=sqc):
                    _w(v, cS, sqc)
                    _ss(v, cV, v.reciprocal(out=c_iv, in_=c_iv))
                    m = SM1[:, slot:slot + 1]
                    ins = v.scalar_tensor_tensor(out=c_nb, in0=m, scalar=-1.0, in1=c_iv,
                                                 op0=ALU.mult, op1=ALU.mult)
                    cV.inc(ins)

                block.vector(p1_inv)
                ivc = cV.n

                ht = H1T[:, (i % 2) * 2048:((i % 2) + 1) * 2048]

                def p1_norm(s, xt=xt, ht=ht, c_iv=c_iv, c_nb=c_nb, ivc=ivc, i=i):
                    _w(s, cV, ivc)
                    if i >= 2:
                        _w(s, cT, h1t_used[i - 2])
                    ins = s.activation(out=ht, in_=xt, func=AF.Identity, scale=c_iv, bias=c_nb)
                    cS.inc(ins)

                block.scalar(p1_norm)
                norm_done[i] = cS.n
                xslot_free[slot] = cS.n
                if i == 0:
                    _maybe_stop('p1_norm0')

                def p1_pre(t, i=i, nd=norm_done[i]):
                    _w(t, cS, nd)
                    if i == 0:
                        _w(t, dY, CONST_DY)

                r = emit_tr16(lambda c, ht=ht: ht[:, c * 128:(c + 1) * 128],
                              lambda c, i=i: a64_h1(c, i * 128, 128), p1_pre)
                h1t_used[i] = r["ct"]
                h1_done[i] = (r["cv"], r["cs"])
                if i == 0:
                    _maybe_stop('p1_tile0')
            _maybe_stop('p1')

            def p2_wk(g):
                g.dma_start(out=WBUF[:, 0:8192].rearrange("p (c j) -> p c j", j=512),
                            in_=wkt.rearrange("(c p) j -> p c j", p=128)).then_inc(dG.h, 16)
                dG.n += 16

            block.gpsimd(p2_wk)
            wk_ld = dG.n

            kproj_mm = [0] * NT
            kbank_free = [None, None]
            kd_ev_done = [0] * NT

            for i in range(NT):
                bank = PJ[i % 2]
                bidx = i % 2

                def p2_mm(t, i=i, bank=bank, bidx=bidx):
                    _w(t, dG, wk_ld)
                    hv, hs = h1_done[i]
                    _w(t, cV, hv)
                    _w(t, cS, hs)
                    _wc(t, kbank_free[bidx])
                    for c in range(16):
                        ins = t.matmul(bank, a64_h1(c, i * 128, 128),
                                       WBUF[:, c * 512:(c + 1) * 512], start=(c == 0),
                                       stop=(c == 15))
                    cT.inc(ins)

                block.tensor(p2_mm)
                kproj_mm[i] = cT.n

                aslot = ATN[:, (i % 2) * 512:((i % 2) + 1) * 512]

                def p2_rope(v, i=i, bank=bank, aslot=aslot, mc=kproj_mm[i]):
                    _w(v, cT, mc)
                    if i == 0:
                        _w(v, dY, CONST_DY)
                    if i >= 2:
                        _w(v, cS, kd_ev_done[i - 2])
                    _ss(v, cV, emit_rope(v, bank, aslot, i))
                    _ss(v, cV, v.scalar_tensor_tensor(out=NSQ, in0=aslot, scalar=1.0, in1=aslot,
                                                      op0=ALU.mult, op1=ALU.mult))
                    _ss(v, cV, v.reduce_sum(out=SM1[:, 0:8],
                                            in_=NSQ.rearrange("p (h j) -> p h j", j=64), axis=AX.X))
                    if i == 0:
                        ins = v.tensor_copy(out=KM2[:], in_=SM1[:, 0:8])
                    else:
                        ins = v.tensor_max(out=KM2[:], in0=KM2[:], in1=SM1[:, 0:8])
                    cV.inc(ins)

                block.vector(p2_rope)
                rp = cV.n
                kbank_free[bidx] = (cV, rp)

                bpeek = pt_g["g"] % 2
                BKkd = PT if bpeek == 0 else PT2

                def p2_kd_tr(t, aslot=aslot, rp=rp):
                    _w(t, cV, rp)
                    _b, BKx = pt_open(t)
                    for g8 in range(8):
                        t.transpose(BKx[0:64, g8 * 128:(g8 + 1) * 128],
                                    aslot[:, g8 * 64:(g8 + 1) * 64], CID[:])
                        ins = t.transpose(BKx[64:128, g8 * 128:(g8 + 1) * 128],
                                          aslot[:, g8 * 64:(g8 + 1) * 64], CID[:],
                                          tile_position=(0, 64))
                        cT.inc(ins)

                block.tensor(p2_kd_tr)
                close = cT.n

                def p2_kd_ev(s, i=i, BKkd=BKkd, close=close):
                    _w(s, cT, close)
                    for g8 in range(8):
                        ins = s.copy(out=akv_kd(g8, i * 128, 128),
                                     in_=BKkd[:, g8 * 128:(g8 + 1) * 128])
                        cS.inc(ins)

                block.scalar(p2_kd_ev)
                pt_g["cons"][bpeek] = [(cS, cS.n)]
                kd_ev_done[i] = cS.n
            kd_done_s = cS.n
            _maybe_stop('p2k')

            def p2_wv(g):
                _w(g, cT, kproj_mm[NT - 1])
                g.dma_start(out=WBUF[:, 0:8192].rearrange("p (c j) -> p c j", j=512),
                            in_=wvt.rearrange("(c p) j -> p c j", p=128)).then_inc(dG.h, 16)
                dG.n += 16

            block.gpsimd(p2_wv)
            wv_ld = dG.n
            vbank_free = [None, None]
            vproj_mm = [0] * NT
            for i in range(NT):
                bank = PJ[2 + i % 2]
                bidx = i % 2

                def p2v_mm(t, i=i, bank=bank, bidx=bidx):
                    _w(t, dG, wv_ld)
                    _wc(t, vbank_free[bidx])
                    for c in range(16):
                        ins = t.matmul(bank, a64_h1(c, i * 128, 128),
                                       WBUF[:, c * 512:(c + 1) * 512], start=(c == 0),
                                       stop=(c == 15))
                    cT.inc(ins)

                block.tensor(p2v_mm)
                vproj_mm[i] = cT.n

                def p2v_ev(v, i=i, bank=bank, mc=vproj_mm[i]):
                    _w(v, cT, mc)
                    if i == 0:
                        _w(v, cG, MEMSET_CG)
                    dst = AKV[:, 16384 + i * 520: 16384 + (i + 1) * 520].rearrange(
                        "p (g j) -> p g j", j=65)[:, :, 0:64]
                    ins = v.tensor_copy(out=dst, in_=bank.rearrange("p (g j) -> p g j", j=64))
                    cV.inc(ins)

                block.vector(p2v_ev)
                vbank_free[bidx] = (cV, cV.n)
            va_done = cV.n
            _maybe_stop('p2')

            wq_free = (cT, vproj_mm[NT - 1])
            qbank_free = [kbank_free[0], kbank_free[1]]
            qtr_done = [0, 0]
            for q in range(4):
                def p3_wq(g, q=q, wq_free=wq_free):
                    _wc(g, wq_free)
                    g.dma_start(out=WBUF[:, 0:8192].rearrange("p (c j) -> p c j", j=512),
                                in_=wqt[:, q * 512:(q + 1) * 512].rearrange("(c p) j -> p c j",
                                                                            p=128)
                                ).then_inc(dG.h, 16)
                    dG.n += 16

                block.gpsimd(p3_wq)
                wq_ld = dG.n

                for tau in range(NS):
                    i = 2 * tau
                    bank = PJ[tau % 2]
                    bidx = tau % 2

                    def p3_mm(t, i=i, bank=bank, bidx=bidx, wq_ld=wq_ld):
                        _w(t, dG, wq_ld)
                        _wc(t, qbank_free[bidx])
                        for cc in range(16):
                            ins = t.matmul(bank, a64_h1(cc, i * 128, 128),
                                           WBUF[:, cc * 512:(cc + 1) * 512],
                                           start=(cc == 0), stop=(cc == 15))
                        cT.inc(ins)

                    block.tensor(p3_mm)
                    qmm = cT.n

                    qb = H1T[:, (tau % 2) * 2048:(tau % 2) * 2048 + 512]

                    def p3_rope(v, q=q, tau=tau, i=i, bank=bank, qb=qb, mc=qmm):
                        _w(v, cT, mc)
                        _w(v, cT, qtr_done[tau % 2])
                        _ss(v, cV, emit_rope(v, bank, qb, i))
                        _ss(v, cV, v.scalar_tensor_tensor(out=NSQ, in0=qb, scalar=1.0, in1=qb,
                                                          op0=ALU.mult, op1=ALU.mult))
                        _ss(v, cV, v.reduce_sum(out=SM1[:, 0:8],
                                                in_=NSQ.rearrange("p (h j) -> p h j", j=64),
                                                axis=AX.X))
                        if tau == 0:
                            ins = v.tensor_copy(out=QM2[:, 8 * q:8 * (q + 1)], in_=SM1[:, 0:8])
                        else:
                            ins = v.tensor_max(out=QM2[:, 8 * q:8 * (q + 1)],
                                               in0=QM2[:, 8 * q:8 * (q + 1)], in1=SM1[:, 0:8])
                        cV.inc(ins)

                    block.vector(p3_rope)
                    rp = cV.n
                    qbank_free[bidx] = (cV, rp)

                    bpeek = pt_g["g"] % 2
                    BKq = PT if bpeek == 0 else PT2

                    def p3_tr(t, qb=qb, rp=rp):
                        _w(t, cV, rp)
                        _b, BKx = pt_open(t)
                        for j in range(4):
                            ins = t.transpose(BKx[:, j * 128:(j + 1) * 128],
                                              qb[:, j * 128:(j + 1) * 128], CID[:])
                            cT.inc(ins)

                    block.tensor(p3_tr)
                    qtr_done[tau % 2] = cT.n
                    close = cT.n

                    def p3_ev(v, q=q, tau=tau, BKq=BKq, close=close):
                        _w(v, cT, close)
                        for j in range(4):
                            ins = v.tensor_copy(out=aq_q(4 * q + j, tau * 128, 128),
                                                in_=BKq[:, j * 128:(j + 1) * 128])
                            cV.inc(ins)

                    block.vector(p3_ev)
                    pt_g["cons"][bpeek] = [(cV, cV.n)]
                wq_free = (cT, cT.n)
            qfm_done = cV.n
            p3_end_ct = cT.n
            _maybe_stop('p3q')

            def p4_v1(v):
                _w(v, cV, qfm_done)
                v.tensor_copy(out=QM2B[:], in_=QM2[:])
                ins = v.tensor_copy(
                    out=KM32B[:].rearrange("p (a b) -> p a b", b=4),
                    in_=KM2[:].unsqueeze(2).broadcast_to([128, 8, 4]))
                cV.inc(ins)

            block.vector(p4_v1)
            c1 = cV.n

            b41 = pt_g["g"] % 2
            BK41 = PT if b41 == 0 else PT2

            def p4_t1(t, c1=c1):
                _w(t, cV, c1)
                _b, BKx = pt_open(t)
                t.transpose(BKx[0:32, 0:128], QM2B[:, 0:32], CID[:])
                ins = t.transpose(BKx[0:32, 128:256], KM32B[:, 0:32], CID[:])
                cT.inc(ins)

            block.tensor(p4_t1)
            c2 = cT.n

            def p4_v2(v, c2=c2):
                _w(v, cT, c2)
                v.reduce_max(out=SM1[0:32, 0:1], in_=BK41[0:32, 0:128], axis=AX.X)
                _ss(v, cV, v.reduce_max(out=SM1[0:32, 1:2], in_=BK41[0:32, 128:256], axis=AX.X))
                ins = v.tensor_mul(out=SM1[0:32, 2:3], in0=SM1[0:32, 0:1], in1=SM1[0:32, 1:2])
                cV.inc(ins)

            block.vector(p4_v2)
            c3 = cV.n
            pt_g["cons"][b41] = [(cV, c3)]

            def p4_s1(s, c3=c3):
                _w(s, cV, c3)
                ins = s.activation(out=SM1[0:32, 3:4], in_=SM1[0:32, 2:3], func=AF.Sqrt,
                                   scale=1.0 / 64.0)
                cS.inc(ins)

            block.scalar(p4_s1)
            c4 = cS.n

            def p4_v3(v, c4=c4):
                _w(v, cS, c4)
                _ss(v, cV, v.tensor_scalar(out=SM1[0:32, 4:5], in0=SM1[0:32, 3:4],
                                           scalar1=-1.0, scalar2=35.0,
                                           op0=ALU.mult, op1=ALU.add))
                ins = v.tensor_copy(out=QM2B[0:32, 0:1], in_=SM1[0:32, 4:5])
                cV.inc(ins)

            block.vector(p4_v3)
            c5 = cV.n

            b42 = pt_g["g"] % 2
            BK42 = PT if b42 == 0 else PT2

            def p4_t2(t, c5=c5):
                _w(t, cV, c5)
                _b, BKx = pt_open(t)
                ins = t.transpose(BKx[0:1, 0:32], QM2B[0:32, 0:1], CID[0:32, 0:32])
                cT.inc(ins)

            block.tensor(p4_t2)
            c6 = cT.n

            def p4_v4(v, c6=c6):
                _w(v, cT, c6)
                ins = v.tensor_copy(out=CAPR[0:1, :], in_=BK42[0:1, 0:32])
                cV.inc(ins)

            block.vector(p4_v4)
            c7 = cV.n
            pt_g["cons"][b42] = [(cV, c7)]

            def p4_t3(t, c7=c7):
                _w(t, cV, c7)
                for c_ in pt_g["cons"][1]:
                    _wc(t, c_)
                ins = t.matmul(PR0[:, 0:32], CONEB[0:1, 0:128], CAPR[0:1, 0:32],
                               start=True, stop=True)
                cT.inc(ins)

            block.tensor(p4_t3)
            c8 = cT.n

            def p4_v5(v, c8=c8):
                _w(v, cT, c8)
                _w(v, dY, CONST_DY)
                _ss(v, cV, v.tensor_copy(out=CAPB[:], in_=PR0[:, 0:32]))
                _ss(v, cV, v.tensor_tensor(out=CAPP[:], in0=CAPB[:, 0:32:2], in1=CAPB[:, 1:32:2],
                                           op=ALU.min))
                ins = v.tensor_tensor(out=CAPZ[:], in0=CAPP[:],
                                      in1=CZF[:, 0:1].broadcast_to([128, 16]), op=ALU.add)
                cV.inc(ins)

            block.vector(p4_v5)
            cap_done = cV.n
            pt_g["cons"][1] = [(cV, cap_done)]

            if debug:
                def pdump1(s):
                    _w(s, cV, cap_done)
                    _w(s, cS, kd_done_s)
                    for dst, src in [(dbgout["h1fm"][:], A64[:, 0:32768]),
                                     (dbgout["qfm"][:], AQ[:, 0:16384]),
                                     (dbgout["kd"][:], AKV[:, 0:16384]),
                                     (dbgout["va"][:], AKV[:, 16384:24704]),
                                     (dbgout["capb"][:], CAPB[:])]:
                        s.dma_start(out=dst, in_=src).then_inc(dY.h, 16)
                        dY.n += 16

                block.sync(pdump1)
                d1 = dY.n

                def pgate1(v, d1=d1):
                    _w(v, dY, d1)
                    ins = v.tensor_copy(out=SM1[:, 0:1], in_=SM1[:, 0:1])
                    cV.inc(ins)

                block.vector(pgate1)
                cap_done = cV.n
            _maybe_stop('p4')

            def visits(gi):
                vs = []
                if gi == 0:
                    for k in range(7):
                        off = ((k + 1) // 2) * 128
                        vs.append((k, off, 512 - off, k % 2 == 0, False))
                    vs.append((15, 0, 512, False, True))
                else:
                    for k in range(9):
                        vs.append((k, 0, 512, k == 8, False))
                    for k in range(9, 15):
                        off = ((k - 7) // 2) * 128
                        vs.append((k, off, 512 - off, k % 2 == 0, False))
                    vs.append((15, 0, 512, False, True))
                return vs

            psab = [PSAB0, PSAB1]
            psab_free = [None, None]
            py_free = [None, None]
            pr_free = [(cV, cap_done), (cV, cap_done)]
            atn_free = [None] * 4
            vseq = [0]
            bst_free = [0]
            linv_free = [0]
            prev_sh = [0]
            prev_rb = [0]
            first_p5 = [True]

            def atn_slot_ap(vidx):
                j = vidx % 4
                if j < 2:
                    return ATN[:, j * 1024:(j + 1) * 1024]
                return SQS[:, (j - 2) * 1024:(j - 1) * 1024]

            for gi in range(2):
                vs = visits(gi)
                nv = len(vs)
                for p in range(16):
                    g = p // 2
                    av_last = 0
                    for vi, (k, off, N, diag, zer) in enumerate(vs):
                        sb = vseq[0] % 2
                        PS = psab[sb]
                        slot = atn_slot_ap(vseq[0])
                        slot_i = vseq[0] % 4
                        vseq[0] += 1

                        def p5_score(t, k=k, off=off, N=N, g=g, p=p, gi=gi, PS=PS, sb=sb):
                            if first_p5[0]:
                                _w(t, cV, cap_done)
                                _w(t, cS, kd_done_s)
                                _w(t, cG, MEMSET_CG)
                                first_p5[0] = False
                            _wc(t, psab_free[sb])
                            qsl = p * 1024 + gi * 512 + off
                            t.matmul(PS[:, off:off + N], akv_kd(g, k * 128, 128, 0, 64),
                                     AQ[0:64, qsl:qsl + N], start=True, stop=True)
                            ins = t.matmul(PS[:, 512 + off:512 + off + N],
                                           akv_kd(g, k * 128, 128, 64, 128),
                                           AQ[64:128, qsl:qsl + N], start=True, stop=True,
                                           tile_position=(64, 0))
                            cT.inc(ins)

                        block.tensor(p5_score)
                        sc_cnt = cT.n

                        def p5_exp(s, off=off, N=N, p=p, PS=PS, slot=slot, slot_i=slot_i,
                                   zer=zer, sc=sc_cnt):
                            _w(s, cT, sc)
                            _wc(s, atn_free[slot_i])
                            bias = (CAPZ if zer else CAPP)[:, p:p + 1]
                            ins = s.activation(
                                out=slot.rearrange("p (b n) -> p b n", b=2)[:, :, 0:N],
                                in_=PS[:].rearrange("p (b n) -> p b n", b=2)[:, :, off:off + N],
                                func=AF.Exp, bias=bias, scale=0.125)
                            cS.inc(ins)

                        block.scalar(p5_exp)
                        psab_free[sb] = (cS, cS.n)
                        exp_cnt = cS.n

                        mask_cnt = None
                        if diag:
                            def p5_mask(v, slot=slot, ec=exp_cnt):
                                _w(v, cS, ec)
                                mv = slot.rearrange("p (b n) -> p b n", b=2)[:, :, 0:128]
                                ins = v.tensor_tensor(
                                    out=mv, in0=mv,
                                    in1=CMSK[:].unsqueeze(1).broadcast_to([128, 2, 128]),
                                    op=ALU.mult)
                                cV.inc(ins)

                            block.vector(p5_mask)
                            mask_cnt = cV.n

                        def p5_av(t, k=k, off=off, N=N, g=g, slot=slot, vi=vi, ec=exp_cnt,
                                  mk=mask_cnt, nv=nv):
                            if mk is not None:
                                _w(t, cV, mk)
                            else:
                                _w(t, cS, ec)
                            if vi == 0:
                                _wc(t, py_free[0])
                                _wc(t, py_free[1])
                            t.matmul(PY0[0:65, off:off + N], akv_va(k, g), slot[:, 0:N],
                                     start=(vi == 0), stop=(vi == nv - 1))
                            ins = t.matmul(PY1[0:65, off:off + N], akv_va(k, g),
                                           slot[:, 512:512 + N],
                                           start=(vi == 0), stop=(vi == nv - 1))
                            cT.inc(ins)

                        block.tensor(p5_av)
                        atn_free[slot_i] = (cT, cT.n)
                        av_last = cT.n

                    def p5_lcopy(v, av=av_last):
                        _w(v, cT, av)
                        _w(v, dS, prev_sh[0])
                        v.tensor_copy(out=LROW[64:65, 0:512], in_=PY0[64:65, 0:512])
                        ins = v.tensor_copy(out=LROW[64:65, 512:1024], in_=PY1[64:65, 0:512])
                        cV.inc(ins)

                    block.vector(p5_lcopy)
                    lc = cV.n

                    def p5_shift(s, lc=lc):
                        _w(s, cV, lc)
                        _w(s, cT, prev_rb[0])
                        s.dma_start(out=LA0, in_=LROW[64:65, 0:512]).then_inc(dS.h, 16)
                        s.dma_start(out=LB0, in_=LROW[64:65, 512:1024]).then_inc(dS.h, 16)
                        dS.n += 32

                    block.scalar(p5_shift)
                    sh = dS.n
                    prev_sh[0] = sh

                    def p5_repl_a(t, sh=sh):
                        _w(t, dS, sh)
                        _wc(t, pr_free[0])
                        ins = t.matmul(PR0[0:64, :], CONEF[0:1, 0:64], LA0, start=True, stop=True)
                        cT.inc(ins)

                    block.tensor(p5_repl_a)
                    ra = cT.n

                    def p5_lninv_a(s, ra=ra):
                        _w(s, cT, ra)
                        _w(s, cV, linv_free[0])
                        _ss(s, cS, s.activation(out=LINV[0:64, :], in_=PR0[0:64, :], func=AF.Ln))
                        ins = s.activation(out=LINV[0:64, :], in_=LINV[0:64, :], func=AF.Exp,
                                           scale=-1.0)
                        cS.inc(ins)

                    block.scalar(p5_lninv_a)
                    la = cS.n
                    pr_free[0] = (cS, la)

                    def p5_evac_a(v, p=p, gi=gi, la=la):
                        _w(v, cS, la)
                        ins = v.scalar_tensor_tensor(out=a64_y(p, gi * 512, 512, 0, 64),
                                                     in0=PY0[0:64, :], scalar=1.0,
                                                     in1=LINV[0:64, :], op0=ALU.mult, op1=ALU.mult)
                        cV.inc(ins)

                    block.vector(p5_evac_a)
                    eva = cV.n
                    py_free[0] = (cV, eva)

                    def p5_repl_b(t, sh=sh):
                        _w(t, dS, sh)
                        _wc(t, pr_free[1])
                        ins = t.matmul(PR1f[0:64, :], CONEF[0:1, 0:64], LB0, start=True, stop=True)
                        cT.inc(ins)

                    block.tensor(p5_repl_b)
                    rb = cT.n
                    prev_rb[0] = rb

                    def p5_lninv_b(s, rb=rb, eva=eva):
                        _w(s, cT, rb)
                        _w(s, cV, eva)
                        _ss(s, cS, s.activation(out=LINV[0:64, :], in_=PR1f[0:64, :], func=AF.Ln))
                        ins = s.activation(out=LINV[0:64, :], in_=LINV[0:64, :], func=AF.Exp,
                                           scale=-1.0)
                        cS.inc(ins)

                    block.scalar(p5_lninv_b)
                    lb = cS.n
                    pr_free[1] = (cS, lb)

                    def p5_evac_b(v, lb=lb):
                        _w(v, cS, lb)
                        _w(v, dB, bst_free[0])
                        ins = v.scalar_tensor_tensor(out=BST[0:64, :], in0=PY1[0:64, :],
                                                     scalar=1.0, in1=LINV[0:64, :],
                                                     op0=ALU.mult, op1=ALU.mult)
                        cV.inc(ins)

                    block.vector(p5_evac_b)
                    evb = cV.n
                    py_free[1] = (cV, evb)
                    linv_free[0] = evb

                    def p5_shift_b(g_, p=p, gi=gi, evb=evb):
                        _w(g_, cV, evb)
                        g_.dma_start(out=a64_y(p, gi * 512, 512, 64, 128), in_=BST[0:64, :]
                                     ).then_inc(dB.h, 16)
                        dB.n += 16

                    block.gpsimd(p5_shift_b)
                    bst_free[0] = dB.n
                if gi == 0:
                    _maybe_stop('p5g0')
            yfm_done_v = cV.n
            yfm_done_g = dB.n
            p5_end_cs = cS.n
            pt_g["cons"][0] = [(cS, p5_end_cs)]
            pt_g["cons"][1] = [(cS, p5_end_cs)]

            if debug:
                def pdump2(s):
                    _w(s, cV, yfm_done_v)
                    _w(s, dB, yfm_done_g)
                    s.dma_start(out=dbgout["yfm"][:], in_=A64[:, 0:16384]).then_inc(dY.h, 16)
                    dY.n += 16

                block.sync(pdump2)
            _maybe_stop('p5')

            wo_free = (cT, p3_end_ct)
            obank_free = [py_free[0], py_free[1]]
            x2_done = [0] * NS
            for dgq in range(4):
                def p6_wo(g, dgq=dgq, wo_free=wo_free):
                    _wc(g, wo_free)
                    g.dma_start(out=WBUF[:, 0:8192].rearrange("p (c j) -> p c j", j=512),
                                in_=wot[:, dgq * 512:(dgq + 1) * 512]
                                .rearrange("(c p) j -> p c j", p=128)).then_inc(dG.h, 16)
                    dG.n += 16

                block.gpsimd(p6_wo)
                wo_ld = dG.n

                for tau in range(NS):
                    bank = PJ[4 + tau % 2]
                    bidx = tau % 2

                    def p6_mm(t, dgq=dgq, tau=tau, bank=bank, bidx=bidx, wo_ld=wo_ld):
                        _w(t, dG, wo_ld)
                        if dgq == 0 and tau == 0:
                            _w(t, cV, yfm_done_v)
                            _w(t, dB, yfm_done_g)
                        _wc(t, obank_free[bidx])
                        for cc in range(16):
                            ins = t.matmul(bank, a64_y(cc, tau * 128, 128),
                                           WBUF[:, cc * 512:(cc + 1) * 512],
                                           start=(cc == 0), stop=(cc == 15))
                        cT.inc(ins)

                    block.tensor(p6_mm)
                    omm = cT.n

                    def p6_ev(v, dgq=dgq, tau=tau, bank=bank, omm=omm):
                        _w(v, cT, omm)
                        if tau >= 6 and dgq == 0:
                            _w(v, dB, yfm_done_g)
                        ins = v.tensor_copy(out=x2loc(tau)[:, dgq * 512:(dgq + 1) * 512], in_=bank)
                        cV.inc(ins)

                    block.vector(p6_ev)
                    obank_free[bidx] = (cV, cV.n)
                    if dgq == 3:
                        xt = x2loc(tau)
                        oe = cV.n

                        def p6_stats(s, xt=xt, oe=oe):
                            _w(s, cV, oe)
                            s.activation(out=SQS[:], in_=xt, func=AF.Square, accum_out=STT[:, 0:1])
                            ins = s.activation(out=SQS[:], in_=xt, func=AF.Copy,
                                               accum_out=STT[:, 1:2])
                            cS.inc(ins)

                        block.scalar(p6_stats)
                        st1 = cS.n

                        def p6_small(v, st1=st1):
                            _w(v, cS, st1)
                            v.tensor_scalar_mul(out=SM1[:, 0:1], in0=STT[:, 1:2], scalar1=1.0 / D)
                            _ss(v, cV, v.tensor_scalar_mul(out=SM1[:, 1:2], in0=STT[:, 0:1],
                                                           scalar1=1.0 / D))
                            _ss(v, cV, v.scalar_tensor_tensor(out=STT[:, 2:3], in0=SM1[:, 0:1],
                                                              scalar=-1.0, in1=SM1[:, 0:1],
                                                              op0=ALU.mult, op1=ALU.mult))
                            ins = v.tensor_add(out=STT[:, 2:3], in0=STT[:, 2:3], in1=SM1[:, 1:2])
                            cV.inc(ins)

                        block.vector(p6_small)
                        sv1 = cV.n

                        def p6_sqrt(s, sv1=sv1):
                            _w(s, cV, sv1)
                            ins = s.activation(out=STT[:, 3:4], in_=STT[:, 2:3], func=AF.Sqrt,
                                               bias=CEPS[:, 0:1])
                            cS.inc(ins)

                        block.scalar(p6_sqrt)
                        st2 = cS.n

                        def p6_norm(v, xt=xt, st2=st2):
                            _w(v, cS, st2)
                            _ss(v, cV, v.reciprocal(out=STT[:, 3:4], in_=STT[:, 3:4]))
                            _ss(v, cV, v.scalar_tensor_tensor(out=STT[:, 4:5], in0=SM1[:, 0:1],
                                                              scalar=-1.0, in1=STT[:, 3:4],
                                                              op0=ALU.mult, op1=ALU.mult))
                            ins = v.tensor_scalar(out=xt, in0=xt, scalar1=STT[:, 3:4],
                                                  scalar2=STT[:, 4:5], op0=ALU.mult, op1=ALU.add)
                            cV.inc(ins)

                        block.vector(p6_norm)
                        nv1 = cV.n

                        def p6_resid(g, xt=xt, tau=tau, nv1=nv1):
                            _w(g, cV, nv1)
                            _w(g, dR, dR.n)
                            g.dma_start(out=xt, in_=xkv[2 * tau * 128:(2 * tau + 1) * 128, :],
                                        accum_op=ALU.add).then_inc(dR.h, 16)
                            dR.n += 16

                        block.gpsimd(p6_resid)
                        x2_done[tau] = dR.n
                wo_free = (cT, cT.n)
            p6_end_ct = cT.n
            _maybe_stop('p6')

            h2_done = [None] * NS
            h2t_used = [0] * NS
            for tau in range(NS):
                xt = x2loc(tau)

                def p7_stats(s, xt=xt, tau=tau):
                    _w(s, dR, x2_done[tau])
                    s.activation(out=SQS[:], in_=xt, func=AF.Square, accum_out=STT[:, 0:1])
                    ins = s.activation(out=SQS[:], in_=xt, func=AF.Copy, accum_out=STT[:, 1:2])
                    cS.inc(ins)

                block.scalar(p7_stats)
                st1 = cS.n

                def p7_small(v, st1=st1):
                    _w(v, cS, st1)
                    v.tensor_scalar_mul(out=SM1[:, 0:1], in0=STT[:, 1:2], scalar1=1.0 / D)
                    _ss(v, cV, v.tensor_scalar_mul(out=SM1[:, 1:2], in0=STT[:, 0:1],
                                                   scalar1=1.0 / D))
                    _ss(v, cV, v.scalar_tensor_tensor(out=STT[:, 2:3], in0=SM1[:, 0:1],
                                                      scalar=-1.0, in1=SM1[:, 0:1],
                                                      op0=ALU.mult, op1=ALU.mult))
                    ins = v.tensor_add(out=STT[:, 2:3], in0=STT[:, 2:3], in1=SM1[:, 1:2])
                    cV.inc(ins)

                block.vector(p7_small)
                sv1 = cV.n

                def p7_sqrt(s, sv1=sv1):
                    _w(s, cV, sv1)
                    ins = s.activation(out=STT[:, 3:4], in_=STT[:, 2:3], func=AF.Sqrt,
                                       bias=CEPS[:, 0:1])
                    cS.inc(ins)

                block.scalar(p7_sqrt)
                st2 = cS.n

                def p7_inv(v, st2=st2):
                    _w(v, cS, st2)
                    _ss(v, cV, v.reciprocal(out=STT[:, 3:4], in_=STT[:, 3:4]))
                    ins = v.scalar_tensor_tensor(out=STT[:, 4:5], in0=SM1[:, 0:1], scalar=-1.0,
                                                 in1=STT[:, 3:4], op0=ALU.mult, op1=ALU.mult)
                    cV.inc(ins)

                block.vector(p7_inv)
                iv1 = cV.n

                ht = H1T[:, (tau % 2) * 2048:((tau % 2) + 1) * 2048]

                def p7_norm(s, xt=xt, ht=ht, iv1=iv1, tau=tau):
                    _w(s, cV, iv1)
                    if tau >= 2:
                        _w(s, cT, h2t_used[tau - 2])
                    ins = s.activation(out=ht, in_=xt, func=AF.Identity, scale=STT[:, 3:4],
                                       bias=STT[:, 4:5])
                    cS.inc(ins)

                block.scalar(p7_norm)
                nd = cS.n

                def p7_pre(t, nd=nd):
                    _w(t, cS, nd)

                r = emit_tr16(lambda c, ht=ht: ht[:, c * 128:(c + 1) * 128],
                              lambda c, tau=tau: aq_q(c, tau * 128, 128), p7_pre)
                h2t_used[tau] = r["ct"]
                h2_done[tau] = (r["cv"], r["cs"])

            if debug:
                def pdump3(s):
                    _w(s, cV, h2_done[NS - 1][0])
                    _w(s, cS, h2_done[NS - 1][1])
                    s.dma_start(out=dbgout["h2fm"][:], in_=AQ[:, 0:16384]).then_inc(dY.h, 16)
                    dY.n += 16
                    for tau in range(NS):
                        s.dma_start(out=dbgout["x2"][:, tau * D:(tau + 1) * D], in_=x2loc(tau)
                                    ).then_inc(dY.h, 16)
                        dY.n += 16

                block.sync(pdump3)
                d3 = dY.n

                def pgate3(v, d3=d3):
                    _w(v, dY, d3)
                    ins = v.tensor_copy(out=SM1[:, 0:1], in_=SM1[:, 0:1])
                    cV.inc(ins)

                block.vector(pgate3)
                h2_done = [(cV.n, h2_done[t_][1]) for t_ in range(NS)]
            _maybe_stop('p7')

            dn_mm_last = 0
            hid_free_cnt = 0
            gu_free = [[(cT, p6_end_ct)] * 2 for _ in range(2)]
            gate_pfree = [psab_free[0], psab_free[0]]
            up_pfree = [psab_free[1], psab_free[1]]
            gslot_free = [atn_free[2], atn_free[3]]
            uslot_free = [atn_free[0], atn_free[1]]
            d_bank_free = [obank_free[0], obank_free[1]]
            ffb_free = None
            outtile_free = None
            hid_written = [0] * NHC
            out_dY = 0
            for tg in range(2):
                h2v = max(h2_done[4 * tg + j][0] for j in range(4))
                h2s = max(h2_done[4 * tg + j][1] for j in range(4))
                if tg == 1:
                    gu_free = [[(cT, dn_mm_last)] * 2 for _ in range(2)]
                gu_mm = [0] * NHC
                for hc in range(NHC):
                    slot = hc % 2

                    wsem = dG if slot == 0 else dW2

                    def p8_wload(g, hc=hc, slot=slot, wsem=wsem):
                        for mi, (mat, base) in enumerate([(gt, 0), (ut, 4096)]):
                            _wc(g, gu_free[mi][slot])
                            src = mat[:, hc * 128:(hc + 1) * 128].rearrange("(c p) h -> p c h",
                                                                            p=128)
                            g.dma_start(out=WBUF[:, base + slot * 2048: base + (slot + 1) * 2048]
                                        .rearrange("p (c h) -> p c h", h=128), in_=src
                                        ).then_inc(wsem.h, 16)
                            wsem.n += 16

                    block.gpsimd(p8_wload)
                    wld = wsem.n

                    if hc % 2 == 0:
                        def p8_sub_v(v, slot=slot, wld=wld, wsem=wsem):
                            _w(v, wsem, wld)
                            v.tensor_scalar_add(out=WBUF[:, slot * 2048:(slot + 1) * 2048],
                                                in0=WBUF[:, slot * 2048:(slot + 1) * 2048],
                                                scalar1=-128.0)
                            ins = v.tensor_scalar_add(
                                out=WBUF[:, 4096 + slot * 2048:4096 + (slot + 1) * 2048],
                                in0=WBUF[:, 4096 + slot * 2048:4096 + (slot + 1) * 2048],
                                scalar1=-128.0)
                            cV.inc(ins)
                        block.vector(p8_sub_v)
                        sub = (cV, cV.n)
                    else:
                        def p8_sub_s(s, slot=slot, wld=wld, wsem=wsem):
                            _w(s, wsem, wld)
                            s.activation(out=WBUF[:, slot * 2048:(slot + 1) * 2048],
                                         in_=WBUF[:, slot * 2048:(slot + 1) * 2048],
                                         func=AF.Identity, bias=CM128[:, 0:1])
                            ins = s.activation(
                                out=WBUF[:, 4096 + slot * 2048:4096 + (slot + 1) * 2048],
                                in_=WBUF[:, 4096 + slot * 2048:4096 + (slot + 1) * 2048],
                                func=AF.Identity, bias=CM128[:, 0:1])
                            cS.inc(ins)
                        block.scalar(p8_sub_s)
                        sub = (cS, cS.n)

                    pg = PJ[0 + slot]
                    pu = PJ[2 + slot]

                    def p8_gu_mm(t, tg=tg, hc=hc, slot=slot, pg=pg, pu=pu, sub=sub,
                                 h2v=h2v, h2s=h2s):
                        _wc(t, sub)
                        if hc == 0:
                            _w(t, cV, h2v)
                            _w(t, cS, h2s)
                        _wc(t, gate_pfree[slot])
                        _wc(t, up_pfree[slot])
                        for cc in range(16):
                            t.matmul(pg,
                                     WBUF[:, slot * 2048 + cc * 128: slot * 2048 + (cc + 1) * 128],
                                     aq_q(cc, tg * 512, 512), start=(cc == 0), stop=(cc == 15))
                        for cc in range(16):
                            ins = t.matmul(pu, WBUF[:, 4096 + slot * 2048 + cc * 128:
                                                    4096 + slot * 2048 + (cc + 1) * 128],
                                           aq_q(cc, tg * 512, 512), start=(cc == 0),
                                           stop=(cc == 15))
                        cT.inc(ins)

                    block.tensor(p8_gu_mm)
                    gu_mm[hc] = cT.n
                    gu_free[0][slot] = (cT, cT.n)
                    gu_free[1][slot] = (cT, cT.n)

                    gslot = SQS[:, slot * 512:(slot + 1) * 512]
                    uslot = ATN[:, slot * 512:(slot + 1) * 512]

                    def p8_gu_ev(s, tg=tg, hc=hc, slot=slot, pg=pg, pu=pu, gslot=gslot,
                                 uslot=uslot, mc=gu_mm[hc]):
                        _w(s, cT, mc)
                        if tg == 0 and hc == 0:
                            _w(s, dY, CONST_DY)
                            _w(s, cS, p5_end_cs)
                        _wc(s, gslot_free[slot])
                        _wc(s, uslot_free[slot])
                        s.activation(out=gslot, in_=pg, func=AF.Silu, scale=CSG[:, hc:hc + 1])
                        ins = s.activation(out=uslot, in_=pu, func=AF.Copy,
                                           scale=CSG[:, 44 + hc:45 + hc])
                        cS.inc(ins)

                    block.scalar(p8_gu_ev)
                    gate_pfree[slot] = (cS, cS.n)
                    up_pfree[slot] = (cS, cS.n)
                    ev_cnt = cS.n

                    def p8_hid(v, tg=tg, hc=hc, gslot=gslot, uslot=uslot, ec=ev_cnt,
                               hfc=hid_free_cnt):
                        _w(v, cS, ec)
                        if tg == 1:
                            _w(v, cT, hfc)
                        ins = v.tensor_tensor(out=a64_hid(hc, 0, 512), in0=gslot, in1=uslot,
                                              op=ALU.mult)
                        cV.inc(ins)

                    block.vector(p8_hid)
                    gslot_free[slot] = (cV, cV.n)
                    uslot_free[slot] = (cV, cV.n)
                    hid_written[hc] = cV.n
                gu_mm_last = gu_mm[NHC - 1]

                dn_free = [(cT, gu_mm_last), (cT, gu_mm_last)]
                for dc in range(NDC):
                    def p8_dload(g, tg=tg, dc=dc, dn_free=dn_free):
                        for half in range(2):
                            _wc(g, dn_free[half])
                            src = dnt[half * 2816:(half + 1) * 2816, dc * 128:(dc + 1) * 128
                                      ].rearrange("(c p) d -> p c d", p=128)
                            g.dma_start(out=WBUF[:, half * 2816:(half + 1) * 2816]
                                        .rearrange("p (c d) -> p c d", d=128), in_=src
                                        ).then_inc(dG.h, 16)
                            dG.n += 16

                    block.gpsimd(p8_dload)
                    dld = dG.n

                    if dc % 2 == 0:
                        def p8_dsub_v(v, dld=dld):
                            _w(v, dG, dld)
                            ins = v.tensor_scalar_add(out=WBUF[:, 0:5632], in0=WBUF[:, 0:5632],
                                                      scalar1=-128.0)
                            cV.inc(ins)
                        block.vector(p8_dsub_v)
                        dsub = (cV, cV.n)
                    else:
                        def p8_dsub_s(s, dld=dld):
                            _w(s, dG, dld)
                            ins = s.activation(out=WBUF[:, 0:5632], in_=WBUF[:, 0:5632],
                                               func=AF.Identity, bias=CM128[:, 0:1])
                            cS.inc(ins)
                        block.scalar(p8_dsub_s)
                        dsub = (cS, cS.n)

                    pd = PJ[4 + dc % 2]

                    def p8_dmm(t, tg=tg, dc=dc, pd=pd, dsub=dsub):
                        _wc(t, dsub)
                        if dc == 0:
                            _w(t, cV, hid_written[NHC - 1])
                        _wc(t, d_bank_free[dc % 2])
                        for hc in range(NHC):
                            ins = t.matmul(pd, WBUF[:, hc * 128:(hc + 1) * 128],
                                           a64_hid(hc, 0, 512), start=(hc == 0),
                                           stop=(hc == NHC - 1))
                        cT.inc(ins)

                    block.tensor(p8_dmm)
                    dmm = cT.n
                    dn_free[0] = (cT, dmm)
                    dn_free[1] = (cT, dmm)

                    def p8_dev(s, dc=dc, pd=pd, dmm=dmm, ffb_free=ffb_free):
                        _w(s, cT, dmm)
                        _wc(s, ffb_free)
                        ins = s.activation(out=a64_ffb(dc, 0, 512), in_=pd, func=AF.Copy,
                                           scale=CSD[:, dc:dc + 1])
                        cS.inc(ins)

                    block.scalar(p8_dev)
                    d_bank_free[dc % 2] = (cS, cS.n)
                dn_mm_last = cT.n
                hid_free_cnt = cT.n
                ffb_done = cS.n

                for j in range(4):
                    tau = 4 * tg + j

                    def p8_pre(t, fd=ffb_done, outtile_free=outtile_free):
                        _w(t, cS, fd)
                        _wc(t, outtile_free)

                    r = emit_tr16(lambda c_, j=j: a64_ffb(c_, j * 128, 128),
                                  lambda c_: OUTTILE[:, c_ * 128:(c_ + 1) * 128], p8_pre)
                    trv = r["cv"]
                    trs = r["cs"]

                    def p8_l4stats(s, trv=trv, trs=trs):
                        _w(s, cV, trv)
                        _w(s, cS, trs)
                        s.activation(out=SQS[:], in_=OUTTILE[:], func=AF.Square,
                                     accum_out=STT[:, 0:1])
                        ins = s.activation(out=SQS[:], in_=OUTTILE[:], func=AF.Copy,
                                           accum_out=STT[:, 1:2])
                        cS.inc(ins)

                    block.scalar(p8_l4stats)
                    st1 = cS.n

                    def p8_l4small(v, st1=st1):
                        _w(v, cS, st1)
                        v.tensor_scalar_mul(out=SM1[:, 0:1], in0=STT[:, 1:2], scalar1=1.0 / D)
                        _ss(v, cV, v.tensor_scalar_mul(out=SM1[:, 1:2], in0=STT[:, 0:1],
                                                       scalar1=1.0 / D))
                        _ss(v, cV, v.scalar_tensor_tensor(out=STT[:, 2:3], in0=SM1[:, 0:1],
                                                          scalar=-1.0, in1=SM1[:, 0:1],
                                                          op0=ALU.mult, op1=ALU.mult))
                        ins = v.tensor_add(out=STT[:, 2:3], in0=STT[:, 2:3], in1=SM1[:, 1:2])
                        cV.inc(ins)

                    block.vector(p8_l4small)
                    sv1 = cV.n

                    def p8_l4sqrt(s, sv1=sv1):
                        _w(s, cV, sv1)
                        ins = s.activation(out=STT[:, 3:4], in_=STT[:, 2:3], func=AF.Sqrt,
                                           bias=CEPS[:, 0:1])
                        cS.inc(ins)

                    block.scalar(p8_l4sqrt)
                    st2 = cS.n

                    def p8_l4norm(v, st2=st2):
                        _w(v, cS, st2)
                        _ss(v, cV, v.reciprocal(out=STT[:, 3:4], in_=STT[:, 3:4]))
                        _ss(v, cV, v.scalar_tensor_tensor(out=STT[:, 4:5], in0=SM1[:, 0:1],
                                                          scalar=-1.0, in1=STT[:, 3:4],
                                                          op0=ALU.mult, op1=ALU.mult))
                        ins = v.tensor_scalar(out=OUTTILE[:], in0=OUTTILE[:], scalar1=STT[:, 3:4],
                                              scalar2=STT[:, 4:5], op0=ALU.mult, op1=ALU.add)
                        cV.inc(ins)

                    block.vector(p8_l4norm)
                    nv1 = cV.n

                    def p8_resid(g, tau=tau, nv1=nv1):
                        _w(g, cV, nv1)
                        _w(g, dR, dR.n)
                        g.dma_start(out=OUTTILE[:], in_=x2loc(tau), accum_op=ALU.add
                                    ).then_inc(dR.h, 16)
                        dR.n += 16

                    block.gpsimd(p8_resid)
                    rg = dR.n

                    def p8_store(s, tau=tau, rg=rg):
                        _w(s, dR, rg)
                        s.dma_start(out=yout[tau * 128:(tau + 1) * 128, :], in_=OUTTILE[:]
                                    ).then_inc(dO.h, 16)
                        dO.n += 16

                    block.sync(p8_store)
                    outtile_free = (dO, dO.n)
                    out_dY = dO.n
                ffb_free = (cT, cT.n)
        except _StopEmit:
            out_dY = dO.n

        def p_final(s, od=out_dY):
            _w(s, dO, od)

        block.sync(p_final)

        real_block.sync(lambda e: _replay(e, RECS["sync"]))
        real_block.gpsimd(lambda e: _replay(e, RECS["gpsimd"]))
        real_block.tensor(lambda e: _replay(e, RECS["tensor"]))
        real_block.vector(lambda e: _replay(e, RECS["vector"]))
        real_block.scalar(lambda e: _replay(e, RECS["scalar"]))

    return nc


def _prep(L0_attn_x, L0_attn_wq, L0_attn_wk, L0_attn_wv, L0_attn_wo,
          L0_ffn_wq_gate, L0_ffn_s_gate, L0_ffn_wq_up, L0_ffn_s_up,
          L0_ffn_wq_down, L0_ffn_s_down):
    x = np.asarray(L0_attn_x, dtype=np.float32)
    shared = {
        "wqt": np.ascontiguousarray(np.asarray(L0_attn_wq, np.float32).T),
        "wkt": np.ascontiguousarray(np.asarray(L0_attn_wk, np.float32).T),
        "wvt": np.ascontiguousarray(np.asarray(L0_attn_wv, np.float32).T),
        "wot": np.ascontiguousarray(np.asarray(L0_attn_wo, np.float32).T),
        "gt": np.ascontiguousarray(np.asarray(L0_ffn_wq_gate, np.int32).T),
        "ut": np.ascontiguousarray(np.asarray(L0_ffn_wq_up, np.int32).T),
        "dnt": np.ascontiguousarray(np.asarray(L0_ffn_wq_down, np.int32).T),
        "sgup": np.ascontiguousarray(np.concatenate(
            [np.asarray(L0_ffn_s_gate, np.float32).reshape(NHC, 128).T,
             np.asarray(L0_ffn_s_up, np.float32).reshape(NHC, 128).T], axis=1)),
        "sdn": np.ascontiguousarray(np.asarray(L0_ffn_s_down, np.float32).reshape(16, 128).T),
        "idn": np.eye(128).astype(BF),
    }
    kk, qq = np.meshgrid(np.arange(128), np.arange(128), indexing="ij")
    shared["dmask"] = (qq >= kk).astype(np.float32).astype(BF)
    theta = 1.0 / (10000.0 ** (np.arange(0, HD, 2, dtype=np.float32) / HD))
    freqs = np.arange(T, dtype=np.float32)[:, None] * theta[None, :]
    cosf = np.cos(freqs).astype(np.float32)
    sinf = np.sin(freqs).astype(np.float32)

    in_maps = []
    for core in range(8):
        seq, role = core // 2, core % 2
        sh = 128 * role
        m = dict(shared)
        m["xkv"] = np.ascontiguousarray(np.roll(x[seq], -sh, axis=0)) if sh else \
            np.ascontiguousarray(x[seq])
        m["cost"] = np.ascontiguousarray(np.roll(cosf, -sh, axis=0)) if sh else cosf
        m["sint"] = np.ascontiguousarray(np.roll(sinf, -sh, axis=0)) if sh else sinf
        m["zflag"] = np.full((128, 1), -50000.0 if role == 0 else 0.0, dtype=np.float32)
        in_maps.append(m)
    return in_maps


def kernel(L0_attn_x, L0_attn_wq, L0_attn_wk, L0_attn_wv, L0_attn_wo,
           L0_ffn_wq_gate, L0_ffn_s_gate, L0_ffn_wq_up, L0_ffn_s_up,
           L0_ffn_wq_down, L0_ffn_s_down):
    in_maps = _prep(L0_attn_x, L0_attn_wq, L0_attn_wk, L0_attn_wv, L0_attn_wo,
                    L0_ffn_wq_gate, L0_ffn_s_gate, L0_ffn_wq_up, L0_ffn_s_up,
                    L0_ffn_wq_down, L0_ffn_s_down)
    nc = build_nc()
    res = run_bass_kernel_spmd(nc, in_maps, core_ids=list(range(8)))
    out = np.empty((4, T, D), dtype=np.float32)
    for core in range(8):
        seq, role = core // 2, core % 2
        yc = np.asarray(res.results[core]["yout"])
        for tau in range(NS):
            jg = 2 * tau + role
            out[seq, jg * 128:(jg + 1) * 128, :] = yc[tau * 128:(tau + 1) * 128, :]
    return out
